# version 16
# baseline (speedup 1.0000x reference)
import math
import contextlib
import numpy as np
import ml_dtypes

import concourse.bass as bass
import concourse.mybir as mybir
from concourse.ap import AP
from concourse.bass_utils import run_bass_kernel_spmd

F32 = mybir.dt.float32
BF16 = mybir.dt.bfloat16
AF = mybir.ActivationFunctionType
OP = mybir.AluOpType
AX = mybir.AxisListType

D = 1024
T = 8192
H = 8
HD = 128
TOK = 4096
NG = 4
GT = 1024
NCH = 64
DH = 4096
TC = 1024
NEG = 256
NQ = 8448
T2C = 8576
ISQ = 1.0 / math.sqrt(HD)
EPS = 1e-5
NEGINF = -1e30

ENGINES = ("tensor", "vector", "scalar", "sync", "gpsimd")


class Plan:
    def __init__(self):
        self.q = {e: [] for e in ENGINES}

    def add(self, e, fn):
        self.q[e].append(fn)


class Sem:
    def __init__(self, nc, st, name):
        self.h = st.enter_context(nc.semaphore(name))
        self.v = 0

    def take(self, k=1):
        self.v += k
        return self.v


class Builder:
    def __init__(self, dbg=(), phases=99, lite=0):
        self.lite = lite
        self.dbg = set(dbg)
        self.phases = phases
        self.nc = bass.Bass("TRN2", target_bir_lowering=False)
        self.st = contextlib.ExitStack()
        self.plan = Plan()
        self._build()

    def E(self, e, meth, *a, wait=None, inc=None, incv=1, **k):
        if inc is not None:
            inc.take(incv)
        def fn(eng, meth=meth, a=a, k=k, wait=wait, inc=inc, incv=incv):
            if wait:
                for s, v in wait:
                    eng.wait_ge(s.h, v)
            inst = getattr(eng, meth)(*a, **k)
            if inc is not None:
                inst.then_inc(inc.h, incv)
        self.plan.add(e, fn)

    def W(self, e, waits):
        def fn(eng, waits=waits):
            for s, v in waits:
                eng.wait_ge(s.h, v)
        self.plan.add(e, fn)

    def barrier(self):
        tot = self.bar.take(4)
        ctv = self.cT.v
        for e in ENGINES:
            if e == "tensor":
                def fn(eng, tot=tot):
                    eng.wait_ge(self.bar.h, tot)
                self.plan.add(e, fn)
            else:
                def fn(eng, tot=tot, ctv=ctv):
                    eng.nop().then_inc(self.bar.h, 1)
                    eng.wait_ge(self.bar.h, tot)
                    eng.wait_ge(self.cT.h, ctv)
                self.plan.add(e, fn)

    def _build(self):
        nc, st = self.nc, self.st

        def din(name, shape, dt=F32):
            return nc.dram_tensor(name, shape, dt, kind="ExternalInput").ap()

        def scr(name, shape, dt):
            return nc.dram_tensor(name, shape, dt, kind="Internal")

        self.x = din("x", [T, D])
        self.xq = din("xq", [TOK, D])
        self.Wq = din("wqm", [D, D]); self.Wk = din("wkm", [D, D])
        self.Wv = din("wvm", [D, D]); self.Wo = din("wom", [D, D])
        self.bo = din("bo", [1, D])
        self.theta = din("theta", [8, 1])
        self.gamma = din("gamma", [128, 8]); self.beta = din("beta", [128, 8])
        self.w1 = din("w1m", [DH, D]); self.w2 = din("w2m", [DH, D])
        self.h1 = din("h1m", [DH, DH]); self.h2 = din("h2m", [DH, DH])
        self.w3 = din("w3m", [D, DH])
        self.s1 = din("s1", [128, 32]); self.s2 = din("s2", [128, 32])
        self.sh1 = din("sh1", [128, 32]); self.sh2 = din("sh2", [128, 32])
        self.w3s = din("w3s", [1, D])
        self.nramp = din("nramp", [8, NQ])
        self.ind = din("ind", [1, NQ])
        self.m2 = din("m2", [128, 640])
        self.o8 = din("o8", [8, 1])
        self.oc = din("oc", [1, 128])
        self.o128f = din("o128f", [128, 1])
        self.o128b = din("o128b", [128, 1], BF16)
        self.y = nc.dram_tensor("y", [TOK, D], F32, kind="ExternalOutput").ap()

        self.xb = scr("xb", [T, D], BF16)
        self.xqb = scr("xqb", [TOK, D], BF16)
        self.dflat = scr("dflat", [NQ], F32)
        self.t2d = scr("t2d", [128 * T2C], F32)
        self.ktS = scr("ktS", [NCH * 128 * 1024], BF16)
        self.vS = scr("vS", [NCH * 128 * 1024], BF16)
        self.qS = scr("qS", [NG * 128 * 8192], BF16)
        self.x2f = scr("x2f", [TOK, D], F32)
        self.x2b = scr("x2b", [TOK, D], BF16)
        self.wbs = {
            "w1": scr("wb1", [DH, D], BF16), "w2": scr("wb2", [DH, D], BF16),
            "h1": scr("wbh1", [DH, DH], BF16), "h2": scr("wbh2", [DH, DH], BF16),
            "w3": scr("wb3", [D, DH], BF16),
        }
        self.wts = {
            "w1": scr("w1t", [32 * 128 * 8 * 128], BF16),
            "w2": scr("w2t", [32 * 128 * 8 * 128], BF16),
            "h1": scr("h1t", [32 * 128 * 32 * 128], BF16),
            "h2": scr("h2t", [32 * 128 * 32 * 128], BF16),
            "w3": scr("w3t", [32 * 128 * 1024], BF16),
        }

        self.dbg_outs = {}
        shapes = {
            "dbg_kt": ([NCH * 128 * 1024], BF16),
            "dbg_v": ([NCH * 128 * 1024], BF16),
            "dbg_q": ([NG * 128 * 8192], BF16),
            "dbg_t2": ([128 * T2C], F32),
            "dbg_o": ([128, 8192], F32),
            "dbg_l": ([1, 8 * GT], F32),
            "dbg_g": ([128, 8192], BF16),
            "dbg_x2": ([TOK, D], F32),
            "dbg_w1t": ([32 * 128 * 8 * 128], BF16),
            "dbg_a": ([128, 8192], F32),
        }
        for name in self.dbg:
            shp, dt = shapes[name]
            self.dbg_outs[name] = nc.dram_tensor(name, shp, dt, kind="ExternalOutput").ap()

        sbuf = lambda name, shape, dt: st.enter_context(nc.sbuf_tensor(name, shape, dt))

        AR_COLS = 73984
        self.arena = sbuf("arena", [128, AR_COLS], BF16)
        arena_f = self.arena.bitcast(F32)

        def cb(c0, n):
            return self.arena[:, c0:c0 + n]

        def cf(c0, n):
            return arena_f[:, c0 // 2:c0 // 2 + n]

        self.T2f = cf(0, T2C)
        self.ind_sb = cf(0, NQ)[0:1, :]
        self.NRf = cf(20736, NQ)[0:8, :]
        self.Dtab_sb = cf(20736, NQ)[0:1, :]
        self.EXf = cf(41216, NQ)[0:8, :]
        self.kstage = [cb(17152, 512), cb(17664, 512)]
        self.vstage = [cb(18176, 1024), cb(19200, 1024)]
        self.lcarv = [cf(17152, 1024), cf(19200, 1024), cf(21248, 1024)]
        self.XTc = [cb(24832, 8192), cb(33024, 8192)]
        self.Wqkv = cb(41216, 16384)
        self.wtstage = [cb(57600, 4096), cb(61696, 4096)]
        self.O_sb = cf(24832, 8192)
        self.Qg_v = cb(41216, 8192)
        self.Kbuf = [cb(49408, 4096), cb(53504, 4096)]
        self.Vbuf = [cb(57600, 4096), cb(61696, 4096)]
        self.GT_v = cb(65792, 8192)
        self.A_v = cb(0, 32768)
        self.B_v = cb(32768, 32768)
        self.Bf_v = cf(32768, 16384)
        self.X2T_v = cb(65536, 8192)

        self.Wo_sb = sbuf("wo_sb", [128, 8192], BF16)
        self.M2sb = sbuf("m2sb", [128, 640], F32)
        self.th_sb = sbuf("th_sb", [8, 1], F32)
        self.nth_sb = sbuf("nth_sb", [8, 1], F32)
        self.o8_sb = sbuf("o8_sb", [8, 1], F32)
        self.oc_sb = sbuf("oc_sb", [1, 128], F32)
        self.o128f_sb = sbuf("o128f_sb", [128, 1], F32)
        self.o128b_sb = sbuf("o128b_sb", [128, 1], BF16)
        self.gm_sb = sbuf("gm_sb", [128, 8], F32)
        self.bt_sb = sbuf("bt_sb", [128, 8], F32)
        self.s_sb = sbuf("s_sb", [128, 128], F32)
        self.bo_rep = sbuf("bo_rep", [128, D], F32)
        self.w3s_rep = sbuf("w3s_rep", [128, D], F32)
        self.bo_row = self.bo_rep[0:1, :]
        self.w3s_row = self.w3s_rep[0:1, :]
        self.SD = [sbuf("sd0", [128, 1024], F32), sbuf("sd1", [128, 1024], F32)]
        self.Pb = [sbuf("pb0", [128, 1024], BF16), sbuf("pb1", [128, 1024], BF16)]
        self.tiles = [sbuf(f"tile{i}", [128, 1024], F32) for i in range(4)]
        self.x2bt = sbuf("x2bt", [128, 1024], BF16)
        self.colA = sbuf("colA", [128, 1], F32)
        self.colB = sbuf("colB", [128, 1], F32)
        self.colC = sbuf("colC", [128, 1], F32)
        self.colD = sbuf("colD", [128, 1], F32)
        self.wbuf = [self.Wo_sb[:, 0:4096], self.Wo_sb[:, 4096:8192]]

        self.ps = [st.enter_context(nc.psum_tensor(f"ps{i}", [128, 1024], F32)) for i in range(4)]

        s = lambda name: Sem(nc, st, name)
        self.dC = s("dC"); self.dG = s("dG"); self.dT = s("dT"); self.dS = s("dS")
        self.dW = s("dW"); self.dO = s("dO")
        self.cT = s("cT"); self.cV = s("cV"); self.cS = s("cS")
        self.vsd = s("vsd"); self.spp = s("spp"); self.tpl = s("tpl")
        self.tS = s("tS"); self.gdn = s("gdn")
        self.vev = s("vev"); self.tblk = s("tblk"); self.bar = s("bar")
        self.tgrp = s("tgrp")

        self.fill_k = 0
        self.evac_hist = []

        self._ph0()
        if not (self.lite & 32):
            self.barrier()
        if self.phases >= 1:
            self._ph1()
        if self.phases >= 2:
            self._ph25()
            self._ph2()
            self.barrier()
        if self.phases >= 3:
            self._ph3()
            self.barrier()
        if self.phases >= 4:
            self._ph4()
        if self.phases >= 5:
            self._ph5()
        self._finish()

    def _ph0(self):
        E, W = self.E, self.W
        dC, dG, dS, cT, cV, cS = self.dC, self.dG, self.dS, self.cT, self.cV, self.cS
        E("scalar", "dma_start", out=self.th_sb[:], in_=self.theta, inc=dC, incv=16)
        th_done = dC.v
        const_list = [] if (self.lite & 1) else None
        for dst, src in ([(self.o8_sb[:], self.o8)] if (self.lite & 1) else
                        [(self.M2sb[:], self.m2), (self.o8_sb[:], self.o8),
                         (self.oc_sb[:], self.oc), (self.o128f_sb[:], self.o128f),
                         (self.o128b_sb[:], self.o128b), (self.bo_row[:], self.bo),
                         (self.w3s_row[:], self.w3s)]) + [(self.ind_sb[:], self.ind)]:
            E("scalar", "dma_start", out=dst, in_=src, inc=dC, incv=16)
        if not (self.lite & 1):
            E("scalar", "dma_start", out=self.gm_sb[:], in_=self.gamma, inc=dC, incv=16)
            E("scalar", "dma_start", out=self.bt_sb[:], in_=self.beta, inc=dC, incv=16)
            for i, t in enumerate([self.s1, self.s2, self.sh1, self.sh2]):
                E("scalar", "dma_start", out=self.s_sb[:, i * 32:(i + 1) * 32],
                  in_=t, inc=dC, incv=16)
        self.c_consts = dC.v
        E("gpsimd", "dma_start", out=self.NRf, in_=self.nramp, inc=dG, incv=16)
        g_nr = dG.v

        if not (self.lite & 4):
            E("vector", "tensor_scalar_mul", self.nth_sb[:], self.th_sb[:], -1.0,
              wait=[(dC, th_done)], inc=cV)
            v_nth = cV.v
            E("scalar", "activation", self.EXf, self.NRf, AF.Exp, scale=self.nth_sb[0:8, 0:1],
              wait=[(dG, g_nr), (cV, v_nth)], inc=cS)
        s_expn = cS.v
        mul_hist = []
        pieces = [] if (self.lite & 8) else [(k * 512, 512) for k in range(NQ // 512)]
        if NQ % 512 and not (self.lite & 8):
            pieces.append((NQ - NQ % 512, NQ % 512))
        for k, (c0, wdt) in enumerate(pieces):
            pk = self.ps[k % 2]
            w = [(cS, s_expn), (dC, self.c_consts)] if k < 2 else [(cV, mul_hist[k - 2])]
            E("tensor", "matmul", pk[0:1, 0:wdt], self.o8_sb[:],
              self.EXf[:, c0:c0 + wdt], start=True, stop=True, wait=w, inc=cT)
            E("vector", "tensor_mul", self.Dtab_sb[:, c0:c0 + wdt],
              pk[0:1, 0:wdt], self.ind_sb[:, c0:c0 + wdt],
              wait=[(cT, cT.v)], inc=cV)
            mul_hist.append(cV.v)
        if self.lite & 16:
            self.s_t2sb = dS.v
            return
        E("gpsimd", "dma_start", out=self.dflat.ap(), in_=self.Dtab_sb,
          wait=[(cV, cV.v)], inc=dG, incv=16)
        E("gpsimd", "dma_start",
          out=AP(self.t2d.ap().tensor, 0, [[T2C + 1, 128], [1, NQ]]),
          in_=AP(self.dflat.ap().tensor, 0, [[0, 128], [1, NQ]]),
          wait=[(dG, dG.v)], inc=dG, incv=16)
        g_t2d = dG.v
        E("scalar", "dma_start", out=self.T2f,
          in_=AP(self.t2d.ap().tensor, 0, [[T2C, 128], [1, T2C]]),
          wait=[(dG, g_t2d)], inc=dS, incv=16)
        self.s_t2sb = dS.v
        if "dbg_t2" in self.dbg_outs:
            E("gpsimd", "dma_start", out=self.dbg_outs["dbg_t2"], in_=self.t2d.ap(),
              wait=[(dG, g_t2d)], inc=dG, incv=16)

        for rep, row, pp in ([] if (self.lite & 2) else [(self.bo_rep, self.bo_row, 2), (self.w3s_rep, self.w3s_row, 3)]):
            for half in range(2):
                E("tensor", "matmul", self.ps[pp][:, half * 512:(half + 1) * 512],
                  self.oc_sb[:], row[0:1, half * 512:(half + 1) * 512],
                  start=True, stop=True, wait=[(dC, self.c_consts)], inc=cT)
                E("vector", "tensor_copy", rep[:, half * 512:(half + 1) * 512],
                  self.ps[pp][:, half * 512:(half + 1) * 512], wait=[(cT, cT.v)], inc=cV)
        if "dbg_o" in self.dbg_outs and self.phases < 1:
            E("gpsimd", "dma_start", out=self.dbg_outs["dbg_o"][:, 0:1024],
              in_=self.bo_rep[:], wait=[(cV, cV.v)], inc=dG, incv=16)
            E("gpsimd", "dma_start", out=self.dbg_outs["dbg_o"][:, 1024:2048],
              in_=self.w3s_rep[:], wait=[(cV, cV.v)], inc=dG, incv=16)

    def _ph1(self):
        E = self.E
        dG = self.dG
        self.g_xb_piece = []
        for i in range(8):
            E("gpsimd", "dma_start", out=self.xb.ap()[i * 1024:(i + 1) * 1024, :],
              in_=self.x[i * 1024:(i + 1) * 1024, :], inc=dG, incv=16)
            self.g_xb_piece.append(dG.v)
        self.g_xb = dG.v
        for i in range(4):
            E("gpsimd", "dma_start", out=self.xqb.ap()[i * 1024:(i + 1) * 1024, :],
              in_=self.xq[i * 1024:(i + 1) * 1024, :], inc=dG, incv=16)
        self.g_xqb = dG.v
        for wsrc, c0 in [(self.Wk, 0), (self.Wv, 8192)]:
            E("gpsimd", "dma_start", out=self.Wqkv[:, c0:c0 + 8192],
              in_=AP(wsrc.tensor, 0, [[1024, 128], [131072, 8], [1, 1024]]), inc=dG, incv=16)
        self.g_wkv = dG.v
        E("gpsimd", "dma_start", out=self.Wo_sb[:],
          in_=AP(self.Wo.tensor, 0, [[1024, 128], [131072, 8], [1, 1024]]), inc=dG, incv=16)
        self.g_wo = dG.v
        if "dbg_o" in self.dbg_outs and self.phases < 2:
            E("gpsimd", "dma_start", out=self.dbg_outs["dbg_o"],
              in_=self.Wo_sb[:], wait=[(dG, self.g_wo)], inc=dG, incv=16)

    def _psum_fill_wait(self):
        k = self.fill_k
        self.fill_k += 1
        if k >= 4:
            return self.ps[k % 4], [(self.cS, self.evac_hist[k - 4])]
        return self.ps[k % 4], []

    def _proj_pass(self, nchunks, src_dram, wcol0, scale, dst_writer, tdone_list):
        E = self.E
        dT, dG, cT, cS = self.dT, self.dG, self.cT, self.cS
        stage_slot_dma = [0, 0]
        for tc in range(nchunks):
            self._pump_prep(25)
            buf = self.XTc[tc % 2]
            w = [(dG, self.g_stream_ready)]
            if tc >= 2:
                w.append((cT, tdone_list[tc - 2]))
            for ib in range(8):
                E("sync", "dma_start", out=buf[:, ib * 1024:(ib + 1) * 1024],
                  in_=src_dram.ap()[tc * 1024:(tc + 1) * 1024, ib * 128:(ib + 1) * 128],
                  transpose=True, wait=w if ib == 0 else None, inc=dT, incv=16)
            tr_done = dT.v
            for o in range(8):
                for tp in range(2):
                    pk, pw = self._psum_fill_wait()
                    pw = pw + [(dT, tr_done)]
                    for i in range(8):
                        E("tensor", "matmul", pk[:, 0:512],
                          self.Wqkv[:, wcol0 + i * 1024 + o * 128:wcol0 + i * 1024 + o * 128 + 128],
                          buf[:, i * 1024 + tp * 512:i * 1024 + tp * 512 + 512],
                          start=(i == 0), stop=(i == 7),
                          wait=pw if i == 0 else None, inc=cT if i == 7 else None)
                    slot = (o * 2 + tp) % 2
                    ew = [(cT, cT.v)]
                    if stage_slot_dma[slot]:
                        ew.append((self.dS, stage_slot_dma[slot]))
                    if scale is None:
                        E("scalar", "copy", self.kstage[slot][:], pk[:, 0:512], wait=ew, inc=cS)
                    else:
                        E("scalar", "activation", self.kstage[slot][:], pk[:, 0:512],
                          AF.Copy, scale=scale, wait=ew, inc=cS)
                    self.evac_hist.append(cS.v)
                    dst_writer(tc, o, tp, self.kstage[slot])
                    stage_slot_dma[slot] = self.dS.v
            tdone_list.append(cT.v)

    def _ph2(self):
        E = self.E
        dG, cT, cS = self.dG, self.cT, self.cS
        self.g_stream_ready = self.g_xb
        tdone = []

        def write_k(tc, o, tp, stage):
            cbase = tc * 8 + tp * 4
            E("gpsimd", "dma_start",
              out=AP(self.ktS.ap().tensor, cbase * 131072 + o * 128,
                     [[1024, 128], [131072, 4], [1, 128]]),
              in_=stage[:], wait=[(cS, cS.v)], inc=dG, incv=16)

        dT = self.dT
        kslot_dma = [0, 0]
        vslot_dma = [0, 0]
        for tc in range(8):
            self._pump_prep(25)
            buf = self.XTc[tc % 2]
            w = [(dG, self.g_xb_piece[tc]), (dG, self.g_wkv)]
            if tc >= 2:
                w.append((cT, tdone[tc - 2]))
            for ib in range(8):
                E("sync", "dma_start", out=buf[:, ib * 1024:(ib + 1) * 1024],
                  in_=self.xb.ap()[tc * 1024:(tc + 1) * 1024, ib * 128:(ib + 1) * 128],
                  transpose=True, wait=w if ib == 0 else None, inc=dT, incv=16)
            tr_done = dT.v
            for o in range(8):
                for tp in range(2):
                    pk, pw = self._psum_fill_wait()
                    pw = pw + [(dT, tr_done)]
                    for i in range(8):
                        E("tensor", "matmul", pk[:, 0:512],
                          self.Wqkv[:, i * 1024 + o * 128:i * 1024 + o * 128 + 128],
                          buf[:, i * 1024 + tp * 512:i * 1024 + tp * 512 + 512],
                          start=(i == 0), stop=(i == 7),
                          wait=pw if i == 0 else None, inc=cT if i == 7 else None)
                    slot = tp
                    ew = [(cT, cT.v)]
                    if kslot_dma[slot]:
                        ew.append((self.dS, kslot_dma[slot]))
                    E("scalar", "copy", self.kstage[slot][:], pk[:, 0:512], wait=ew, inc=cS)
                    self.evac_hist.append(cS.v)
                    cbase = tc * 8 + tp * 4
                    E("scalar", "dma_start",
                      out=AP(self.ktS.ap().tensor, cbase * 131072 + o * 128,
                             [[1024, 128], [131072, 4], [1, 128]]),
                      in_=self.kstage[slot][:], inc=self.dS, incv=16)
                    kslot_dma[slot] = self.dS.v
            for tt in range(8):
                pk, pw = self._psum_fill_wait()
                for oc_i in range(2):
                    for i in range(8):
                        E("tensor", "matmul", pk[:, oc_i * 512:(oc_i + 1) * 512],
                          buf[:, i * 1024 + tt * 128:i * 1024 + tt * 128 + 128],
                          self.Wqkv[:, 8192 + i * 1024 + oc_i * 512:8192 + i * 1024 + oc_i * 512 + 512],
                          start=(i == 0), stop=(i == 7),
                          wait=pw if (i == 0 and oc_i == 0) else None,
                          inc=cT if (i == 7 and oc_i == 1) else None)
                slot = tt % 2
                ew = [(cT, cT.v)]
                if vslot_dma[slot]:
                    ew.append((self.dS, vslot_dma[slot]))
                E("scalar", "copy", self.vstage[slot][:], pk[:], wait=ew, inc=cS)
                self.evac_hist.append(cS.v)
                cix = tc * 8 + tt
                E("scalar", "dma_start", out=self.vS.ap()[cix * 131072:(cix + 1) * 131072],
                  in_=self.vstage[slot][:], inc=self.dS, incv=16)
                vslot_dma[slot] = self.dS.v
            tdone.append(cT.v)
        self.g_kv_written = None
        self.g_kv_written_dS = self.dS.v
        if "dbg_kt" in self.dbg_outs:
            E("gpsimd", "dma_start", out=self.dbg_outs["dbg_kt"], in_=self.ktS.ap(),
              wait=[(dG, self.g_kv_written)], inc=dG, incv=16)
        if "dbg_v" in self.dbg_outs:
            E("gpsimd", "dma_start", out=self.dbg_outs["dbg_v"], in_=self.vS.ap(),
              wait=[(dG, self.g_kv_written)], inc=dG, incv=16)

    def _ph3(self):
        E = self.E
        dG, cS = self.dG, self.cS
        E("gpsimd", "dma_start", out=self.Wqkv[:, 0:8192],
          in_=AP(self.Wq.tensor, 0, [[1024, 128], [131072, 8], [1, 1024]]), inc=dG, incv=16)
        self.g_stream_ready = dG.v
        tdone = []

        def write_q(tc, o, tp, stage):
            E("scalar", "dma_start",
              out=AP(self.qS.ap().tensor, tc * 1048576 + o * 1024 + tp * 512,
                     [[8192, 128], [1, 512]]),
              in_=stage[:], inc=self.dS, incv=16)

        self._proj_pass(4, self.xqb, 0, ISQ, write_q, tdone)
        self._prep_finalize()
        self.g_q_written_dS = self.dS.v
        if "dbg_q" in self.dbg_outs:
            E("gpsimd", "dma_start", out=self.dbg_outs["dbg_q"], in_=self.qS.ap(),
              wait=[(dG, self.g_q_written)], inc=dG, incv=16)



    def _ph25(self):
        E = self.E
        dG = self.dG
        dT = self.dT
        dS = self.dS
        items = []
        cast_done = {}

        def mk_cast(nm, src_t, r0, r1):
            def f():
                E("gpsimd", "dma_start", out=self.wbs[nm].ap()[r0:r1, :],
                  in_=src_t[r0:r1, :], inc=dG, incv=16)
                cast_done[nm] = dG.v
            return f

        self._stage_dma = [0, 0]
        self._prep_k = 0

        def mk_tr(nm, ib, nib, nob, obstride, rows):
            def f():
                slot = self._prep_k % 2
                st_v = self.wtstage[slot][:, 0:rows]
                w = [(dG, cast_done[nm])]
                if self._stage_dma[slot]:
                    w.append((dS, self._stage_dma[slot]))
                E("sync", "dma_start", out=st_v,
                  in_=self.wbs[nm].ap()[:, ib * 128:(ib + 1) * 128],
                  transpose=True, wait=w, inc=dT, incv=16)
                tr = dT.v
                if nm == "w3":
                    E("scalar", "dma_start",
                      out=self.wts[nm].ap()[ib * 131072:(ib + 1) * 131072],
                      in_=st_v, wait=[(dT, tr)], inc=dS, incv=16)
                else:
                    E("scalar", "dma_start",
                      out=AP(self.wts[nm].ap().tensor, ib * 128,
                             [[nib * 128, 128], [obstride, nob], [1, 128]]),
                      in_=st_v, wait=[(dT, tr)], inc=dS, incv=16)
                self._stage_dma[slot] = dS.v
                self._prep_k += 1
            return f

        cfg = {"w1": (self.w1, DH, D, 8, 32, 131072), "w2": (self.w2, DH, D, 8, 32, 131072),
               "h1": (self.h1, DH, DH, 32, 32, 524288), "h2": (self.h2, DH, DH, 32, 32, 524288),
               "w3": (self.w3, D, DH, 32, 0, 0)}
        cast_items = []
        for nm in ["w1", "w2", "h1", "h2", "w3"]:
            src_t, rows, cols, nib, nob, obstride = cfg[nm]
            npiece = max(1, (rows * cols * 4) // (8 << 20))
            rp = rows // npiece
            for k in range(npiece):
                cast_items.append(mk_cast(nm, src_t, k * rp, (k + 1) * rp))
            for ib in range(nib):
                items.append(mk_tr(nm, ib, nib, nob, obstride, rows))
        for f in cast_items:
            f()
        self._prep_items = items

    def _pump_prep(self, n):
        while n > 0 and self._prep_items:
            self._prep_items.pop(0)()
            n -= 1

    def _prep_finalize(self):
        while self._prep_items:
            self._prep_items.pop(0)()
        self.prep_dG = self.dG.v
        self.prep_dT = self.dT.v
        self.prep_dS = self.dS.v

    def _flush_pv(self, pending, tpl_hist, spp_hist):
        E = self.E
        d = pending.pop(0)
        kslot = d["kk"] % 4
        pbb = self.Pb[kslot // 2][:, (kslot % 2) * 512:(kslot % 2) * 512 + 512]
        w = [(self.spp, spp_hist[d["kk"]])]
        if d["first"]:
            w.append((self.vev, self.vev.v))
        E("tensor", "matmul", self.ps[2][:, d["p0"]:d["p1"]],
          self.Vbuf[d["gb"] % 2][:, d["ci"] * 1024 + d["h"] * 128:d["ci"] * 1024 + d["h"] * 128 + 128],
          pbb[:, 0:d["p1"] - d["p0"]],
          start=(d["ci"] == 0), stop=(d["ci"] == 3), wait=w)
        row = 0 if d["p0"] < 512 else 32
        E("tensor", "matmul", self.ps[3][row:row + 1, d["p0"]:d["p1"]],
          self.o128b_sb[:], pbb[:, 0:d["p1"] - d["p0"]],
          start=(d["ci"] == 0), stop=(d["ci"] == 3),
          tile_position=(0, row), inc=self.tpl)
        tpl_hist.append(self.tpl.v)


    def _flush_evac(self, evac_pending):
        E = self.E
        d = evac_pending.pop(0)
        h, b, vs0 = d["h"], d["b"], d["vs0"]
        w = [(self.tpl, d["tplv"])]
        if b == 0:
            E("vector", "tensor_copy", self.O_sb[:, h * 1024:(h + 1) * 1024],
              self.ps[2][:, 0:1024], wait=w, inc=self.vev)
            E("vector", "tensor_copy", self.lrow(h), self.ps[3][0:1, 0:1024], inc=self.vev)
        else:
            E("vector", "tensor_add",
              self.O_sb[:, h * 1024 + vs0:(h + 1) * 1024],
              self.O_sb[:, h * 1024 + vs0:(h + 1) * 1024],
              self.ps[2][:, vs0:1024], wait=w, inc=self.vev)
            E("vector", "tensor_add", self.lrow(h)[:, vs0:],
              self.lrow(h)[:, vs0:], self.ps[3][0:1, vs0:1024], inc=self.vev)

    def lrow(self, h):
        return self.lcarv[h // 3][32 * (h % 3):32 * (h % 3) + 1, :]


    def _ph4(self):
        E = self.E
        dS, dG, cS, cV, cT = self.dS, self.dG, self.cS, self.cV, self.cT
        self.W("vector", [(dS, self.g_q_written_dS)])
        tS, vsd, spp, tpl, vev = self.tS, self.vsd, self.spp, self.tpl, self.vev
        ps = self.ps
        smult_hist = []
        sfill_hist = []
        svsd_full = []
        spp_hist = []
        tpl_hist = []
        kk = 0
        blk_tpl = []
        kv_dma_hist = []
        grp_tpl = []

        last_gT_cs = 0

        pending = []
        evac_pending = []
        for g in range(NG):
            nch = 16 * g + 16
            nblk = nch // 4
            w = [(dS, self.g_q_written_dS)]
            if g > 0:
                w.append((tpl, grp_tpl[g - 1]))
            E("scalar", "dma_start", out=self.Qg_v[:],
              in_=self.qS.ap()[g * 1048576:(g + 1) * 1048576],
              wait=w, inc=dS, incv=16)
            qg_dma = dS.v

            for b in range(nblk):
                c0 = b * 4
                gb = len(blk_tpl)
                w = [(dS, self.g_kv_written_dS)]
                if gb == 0:
                    w += [(dG, self.prep_dG), (self.dT, self.prep_dT), (dS, self.prep_dS)]
                if gb >= 2:
                    w.append((tpl, blk_tpl[gb - 2]))
                E("scalar", "dma_start", out=self.Kbuf[gb % 2][:],
                  in_=AP(self.ktS.ap().tensor, c0 * 131072 + g * 16 * 131072 * 0,
                         [[1024, 128], [131072, 4], [1, 1024]]),
                  wait=w, inc=dS, incv=16)
                E("scalar", "dma_start", out=self.Vbuf[gb % 2][:],
                  in_=AP(self.vS.ap().tensor, c0 * 131072,
                         [[1024, 128], [131072, 4], [1, 1024]]),
                  inc=dS, incv=16)
                kv_dma_hist.append(dS.v)

                for h in range(8):
                    npieces_head = 0
                    first_cols = None
                    for ci in range(4):
                        c = c0 + ci
                        jmin = max(0, (c - 16 * g) // 2)
                        vs = jmin * 128
                        jm = (c - 16 * g) // 2
                        q0m = 384 if (c - 16 * g) % 2 == 0 else 256
                        pieces = []
                        if vs < 512:
                            pieces.append((vs, 512))
                        pieces.append((max(512, vs), 1024))
                        if first_cols is None:
                            first_cols = pieces[0][0]
                        for (p0, p1) in pieces:
                            kslot = kk % 4
                            sp = self.ps[kslot // 2][:, (kslot % 2) * 512:(kslot % 2) * 512 + 512]
                            sdb = self.SD[kslot // 2][:, (kslot % 2) * 512:(kslot % 2) * 512 + 512]
                            pbb = self.Pb[kslot // 2][:, (kslot % 2) * 512:(kslot % 2) * 512 + 512]
                            w = [(dS, kv_dma_hist[gb]), (dS, qg_dma)]
                            if kk >= 4:
                                w.append((vsd, smult_hist[kk - 4]))
                            if b == 0 and h == 0 and ci == 0 and g > 0 and p0 == pieces[0][0]:
                                w.append((cV, grp_vec_done))
                            E("tensor", "matmul", sp[:, 0:p1 - p0],
                              self.Kbuf[gb % 2][:, ci * 1024 + h * 128:ci * 1024 + h * 128 + 128],
                              self.Qg_v[:, h * 1024 + p0:h * 1024 + p1],
                              start=True, stop=True, wait=w, inc=tS)
                            sfill_hist.append(tS.v)
                            q0 = (16 * g + 1 - c) * 128 + NEG
                            j0 = p0 // 128
                            nj = (p1 - p0) // 128
                            t2v = self.T2f[:, q0 + j0 * 256:q0 + j0 * 256 + nj * 256]
                            t2v = t2v.rearrange("p (j f) -> p j f", f=256)[:, :, 0:128]
                            sview = sp[:, 0:p1 - p0].rearrange("p (j f) -> p j f", f=128)
                            w = [(tS, sfill_hist[kk])]
                            if kk >= 4:
                                w.append((spp, spp_hist[kk - 4]))
                            E("vector", "tensor_mul",
                              sdb[:, 0:p1 - p0].rearrange("p (j f) -> p j f", f=128),
                              sview, t2v, wait=w, inc=vsd)
                            if p0 <= jm * 128 < p1:
                                E("vector", "tensor_add",
                                  sdb[:, jm * 128 - p0:jm * 128 - p0 + 128],
                                  sdb[:, jm * 128 - p0:jm * 128 - p0 + 128],
                                  self.M2sb[:, q0m:q0m + 128], inc=vsd)
                            smult_hist.append(vsd.v)
                            w = [(vsd, smult_hist[kk])]
                            if kk >= 4:
                                w.append((tpl, tpl_hist[kk - 4]))
                            E("scalar", "activation", pbb[:, 0:p1 - p0], sdb[:, 0:p1 - p0],
                              AF.Exp, wait=w, inc=spp)
                            spp_hist.append(spp.v)
                            pending.append(dict(gb=gb, ci=ci, p0=p0, p1=p1, h=h,
                                                kk=kk, first=(ci == 0 and p0 == first_cols)))
                            while len(pending) > 3:
                                self._flush_pv(pending, tpl_hist, spp_hist)
                            kk += 1
                            npieces_head += 1
                            if npieces_head == 2 and evac_pending:
                                self._flush_evac(evac_pending)
                    vs0 = max(0, (c0 - 16 * g) // 2) * 128
                    evac_pending.append(dict(h=h, b=b, vs0=vs0,
                                             tplv=tpl.v + len(pending)))
                blk_tpl.append(tpl.v + len(pending))
            while pending:
                self._flush_pv(pending, tpl_hist, spp_hist)
            while evac_pending:
                self._flush_evac(evac_pending)
            grp_tpl.append(tpl.v)
            grp_vev = vev.v

            if "dbg_o" in self.dbg_outs and g == 0:
                E("gpsimd", "dma_start", out=self.dbg_outs["dbg_o"],
                  in_=self.O_sb, wait=[(vev, grp_vev)], inc=dG, incv=16)
                for h in range(8):
                    E("gpsimd", "dma_start",
                      out=self.dbg_outs["dbg_l"][:, h * 1024:(h + 1) * 1024],
                      in_=self.lrow(h), wait=[(vev, grp_vev)], inc=dG, incv=16)

            stats_free = [None, None]
            bc_free = [None, None]
            gt_hist = []
            sq_big_cs = {}
            for h in range(8):
                pr = h % 2
                rA = self.SD[pr][0:1, :]
                rB = self.tiles[pr][0:1, :]
                rC = self.tiles[2 + pr][0:1, :]
                sqt = self.tiles[2 + pr]
                hc = slice(h * 1024, (h + 1) * 1024)
                w = [(vev, grp_vev)]
                if h >= 2:
                    w.append((cT, sq_free[pr]))
                if h >= 2:
                    w.append((cV, gt_hist[h - 2]))
                E("scalar", "activation", sqt[:], self.O_sb[:, hc], AF.Square,
                  wait=w, inc=cS)
                sq_big_cs[h] = cS.v
                w = [(vev, grp_vev), (cS, sq_big_cs[h])]
                if h >= 1 and stats_free[0] is not None:
                    w.append((cS, stats_free[0]))
                    w.append((cV, stats_free[1]))
                if h == 0:
                    w.append((vsd, smult_hist[kk - 1]))
                for pi, (p0, p1) in enumerate([(0, 512), (512, 1024)]):
                    E("tensor", "matmul", ps[0][0:1, p0:p1], self.o128f_sb[:],
                      self.O_sb[:, h * 1024 + p0:h * 1024 + p1],
                      start=True, stop=True, wait=w if pi == 0 else None)
                for pi, (p0, p1) in enumerate([(0, 512), (512, 1024)]):
                    E("tensor", "matmul", ps[1][0:1, p0:p1], self.o128f_sb[:],
                      sqt[:, p0:p1], start=True, stop=True,
                      inc=cT if pi == 1 else None)
                stats_ct = cT.v
                if h >= 2:
                    pass
                sq_free_prev = sq_free[pr] if h >= 2 else None
                E("scalar", "copy", rC, ps[0][0:1, :], wait=[(cT, stats_ct)], inc=cS)
                E("scalar", "activation", rA, ps[0][0:1, :], AF.Square, inc=cS)
                sq_cs = cS.v
                E("vector", "tensor_sub", rA, ps[1][0:1, :], rA,
                  wait=[(cS, sq_cs)], inc=cV)
                var_cv = cV.v
                stats_free = [sq_cs, var_cv]
                E("scalar", "activation", rB, self.lrow(h), AF.Square, inc=cS)
                lsq_cs = cS.v
                E("vector", "scalar_tensor_tensor", rB, rB, EPS, rA,
                  OP.mult, OP.add, wait=[(cS, lsq_cs), (cV, var_cv)], inc=cV)
                den_cv = cV.v
                E("scalar", "activation", rA, rB, AF.Ln, wait=[(cV, den_cv)], inc=cS)
                E("scalar", "activation", rB, rA, AF.Exp, scale=-0.5, inc=cS)
                rstd_cs = cS.v
                w = [(cS, sq_cs)]
                if bc_free[0] is not None:
                    w.append((cV, bc_free[0]))
                for pi, (p0, p1) in enumerate([(0, 512), (512, 1024)]):
                    E("tensor", "matmul", ps[2][:, p0:p1], self.oc_sb[:],
                      rC[:, p0:p1], start=True, stop=True,
                      wait=w if pi == 0 else None)
                w2 = [(cS, rstd_cs)]
                if bc_free[1] is not None:
                    w2.append((cV, bc_free[1]))
                for pi, (p0, p1) in enumerate([(0, 512), (512, 1024)]):
                    E("tensor", "matmul", ps[3][:, p0:p1], self.oc_sb[:],
                      rB[:, p0:p1], start=True, stop=True,
                      wait=w2 if pi == 0 else None, inc=cT if pi == 1 else None)
                bc_ct = cT.v
                E("vector", "tensor_sub", sqt[:], self.O_sb[:, hc],
                  ps[2][:, 0:1024], wait=[(cT, bc_ct)], inc=cV)
                sub_cv = cV.v
                E("vector", "tensor_mul", sqt[:], sqt[:], ps[3][:, 0:1024], inc=cV)
                mult_cv = cV.v
                bc_free = [sub_cv, mult_cv]
                E("vector", "tensor_scalar", self.GT_v[:, hc], sqt[:],
                  self.gm_sb[:, h:h + 1], self.bt_sb[:, h:h + 1], OP.mult, OP.add,
                  inc=cV)
                gt_hist.append(cV.v)
                sq_free = getattr(self, "_gn_sqfree", [None, None])
                sq_free[pr] = stats_ct
                self._gn_sqfree = sq_free
            last_gT_cs = cV.v
            self._gn_sqfree = [None, None]
            woev_hist = []
            prev_x2_cs = None
            prev_dg = 0
            for tt in range(8):
                pk = ps[tt % 2]
                xqt = self.tiles[2 + tt % 2]
                w = []
                if tt >= 2:
                    w.append((cV, woev_hist[tt - 2][0]))
                E("scalar", "dma_start", out=xqt[:],
                  in_=self.xq[(g * 1024 + tt * 128):(g * 1024 + (tt + 1) * 128), :],
                  wait=w if w else None, inc=dS, incv=16)
                xq_dma = dS.v
                w = [(cV, last_gT_cs)]
                if self.lite & 128:
                    w += [(vev, vev.v), (spp, spp.v), (vsd, vsd.v), (tpl, tpl.v),
                          (tS, tS.v), (cS, cS.v), (dS, dS.v), (dG, self.dG.v)]
                if tt >= 2:
                    w.append((cV, woev_hist[tt - 2][1]))
                first = True
                for hh in range(8):
                    for oc_i in range(2):
                        E("tensor", "matmul", pk[:, oc_i * 512:(oc_i + 1) * 512],
                          self.GT_v[:, hh * 1024 + tt * 128:hh * 1024 + tt * 128 + 128],
                          self.Wo_sb[:, hh * 1024 + oc_i * 512:hh * 1024 + oc_i * 512 + 512],
                          start=(hh == 0), stop=(hh == 7),
                          wait=w if first else None,
                          inc=cT if (hh == 7 and oc_i == 1) else None)
                        first = False
                wo_ct = cT.v
                wv = [(cT, wo_ct), (dS, xq_dma)]
                if prev_x2_cs is not None:
                    wv.append((cS, prev_x2_cs))
                    wv.append((dG, prev_dg))
                E("vector", "tensor_add", self.tiles[1][:], pk[:, 0:1024],
                  self.bo_rep[:], wait=wv, inc=cV)
                E("vector", "tensor_add", self.tiles[0][:], self.tiles[1][:],
                  xqt[:], inc=cV)
                ev0 = cV.v
                dbg_wait = []
                if "dbg_a" in self.dbg_outs and g == 0:
                    E("gpsimd", "dma_start",
                      out=self.dbg_outs["dbg_a"][:, tt * 1024:(tt + 1) * 1024],
                      in_=self.tiles[0][:], wait=[(cV, ev0)], inc=dG, incv=16)
                    dbg_wait = [(dG, dG.v)]
                E("vector", "reduce_sum", self.colA[:], self.tiles[0][:], axis=AX.X, inc=cV)
                E("vector", "tensor_scalar_mul", self.colA[:], self.colA[:],
                  -1.0 / 1024, wait=[(cV, cV.v)], inc=cV)
                neg_cv = cV.v
                E("scalar", "add", self.tiles[1][:], self.tiles[0][:],
                  self.colA[:, 0:1], wait=[(cV, neg_cv)] + dbg_wait, inc=cS)
                c_cs = cS.v
                E("vector", "tensor_mul", self.tiles[0][:], self.tiles[1][:],
                  self.tiles[1][:], wait=[(cS, c_cs)], inc=cV)
                E("vector", "reduce_sum", self.colB[:], self.tiles[0][:], axis=AX.X, inc=cV)
                E("vector", "tensor_scalar", self.colB[:], self.colB[:],
                  1.0 / 1024, EPS, OP.mult, OP.add, wait=[(cV, cV.v)], inc=cV)
                var_cv = cV.v
                dwait2 = []
                if "dbg_l" in self.dbg_outs and g == 0 and tt == 0:
                    E("gpsimd", "dma_start", out=AP(self.dbg_outs["dbg_l"].tensor, 0, [[1, 128], [1, 1]]),
                      in_=self.colA[:], wait=[(cV, neg_cv), (cS, cS.v)], inc=dG, incv=16)
                    E("gpsimd", "dma_start", out=AP(self.dbg_outs["dbg_l"].tensor, 128, [[1, 128], [1, 1]]),
                      in_=self.colB[:], wait=[(cV, var_cv)], inc=dG, incv=16)
                    dwait2 = [(dG, dG.v)]
                E("scalar", "activation", self.colA[:], self.colB[:], AF.Sqrt,
                  wait=[(cV, var_cv)] + dwait2, inc=cS)
                E("vector", "reciprocal", self.colB[:], self.colA[:],
                  wait=[(cS, cS.v)], inc=cV)
                rstd_cv = cV.v
                if "dbg_l" in self.dbg_outs and g == 0 and tt == 0:
                    E("gpsimd", "dma_start", out=AP(self.dbg_outs["dbg_l"].tensor, 256, [[1, 128], [1, 1]]),
                      in_=self.colB[:], wait=[(cV, rstd_cv)], inc=dG, incv=16)
                    dwait3 = [(dG, dG.v)]
                else:
                    dwait3 = []
                E("scalar", "mul", self.tiles[0][:], self.tiles[1][:],
                  self.colB[:, 0:1], wait=[(cV, rstd_cv)] + dwait3, inc=cS)
                x2_cs = cS.v
                E("vector", "tensor_copy", self.x2bt[:], self.tiles[0][:],
                  wait=[(cS, x2_cs), (dG, dG.v)], inc=cV)
                ev1 = cV.v
                row0 = g * 1024 + tt * 128
                E("gpsimd", "dma_start", out=self.x2f.ap()[row0:row0 + 128, :],
                  in_=self.tiles[0][:], wait=[(cV, ev1)], inc=dG, incv=16)
                E("gpsimd", "dma_start", out=self.x2b.ap()[row0:row0 + 128, :],
                  in_=self.x2bt[:], wait=[(cV, ev1)], inc=dG, incv=16)
                woev_hist.append((ev0, ev1))
                prev_x2_cs = x2_cs
                prev_dg = dG.v
            grp_vec_done = cV.v
        self.g_x2_written = dG.v
        if "dbg_x2" in self.dbg_outs:
            E("gpsimd", "dma_start", out=self.dbg_outs["dbg_x2"], in_=self.x2f.ap(),
              wait=[(dG, self.g_x2_written)], inc=dG, incv=16)


    def _ph5(self):
        E = self.E
        dS, dG, dT, dO = self.dS, self.dG, self.dT, self.dO
        cS, cV, cT = self.cS, self.cV, self.cT
        ps = self.ps
        wb = lambda s: self.Wo_sb[:, s * 4096:(s + 1) * 4096]

        ffnA = self.tiles[2]
        ffnB = self.tiles[3]
        ph4_ct = cT.v
        wdma_hist = [ph4_ct, ph4_ct]
        ab_mult_hist = cV.v
        ln2_hist = [(cS.v, 0), (cS.v, 0)]
        last_w3_cv = cV.v

        prev_w3_region = [cV.v] * 4
        for ch in range(4):
            w = [(dG, self.g_x2_written)]
            for ib in range(8):
                E("sync", "dma_start", out=self.X2T_v[:, ib * 1024:(ib + 1) * 1024],
                  in_=self.x2b.ap()[ch * 1024:(ch + 1) * 1024, ib * 128:(ib + 1) * 128],
                  transpose=True, wait=w if ib == 0 else None, inc=dT, incv=16)
            x2t_dT = dT.v

            for ob in range(32):
                slot = ob % 2
                E("sync", "dma_start", out=wb(slot)[:, 0:1024],
                  in_=self.wts["w1"].ap()[ob * 131072:(ob + 1) * 131072],
                  wait=[(cT, wdma_hist[slot])], inc=dT, incv=16)
                E("sync", "dma_start", out=wb(slot)[:, 1024:2048],
                  in_=self.wts["w2"].ap()[ob * 131072:(ob + 1) * 131072],
                  inc=dT, incv=16)
                wdT = dT.v
                fills = []
                for wi in range(2):
                    pk, pw = self._psum_fill_wait()
                    pw = pw + [(dT, wdT), (dT, x2t_dT)]
                    if ch == 0 and ob == 0 and wi == 0:
                        pw.append((cV, ab_mult_hist))
                    if ch > 0 and ob < 2:
                        pw.append((cV, prev_w3_region[self.ps.index(pk)]))
                    for tp in range(2):
                        for ib in range(8):
                            E("tensor", "matmul", pk[:, tp * 512:(tp + 1) * 512],
                              wb(slot)[:, wi * 1024 + ib * 128:wi * 1024 + ib * 128 + 128],
                              self.X2T_v[:, ib * 1024 + tp * 512:ib * 1024 + tp * 512 + 512],
                              start=(ib == 0), stop=(ib == 7),
                              wait=pw if (tp == 0 and ib == 0) else None,
                              inc=cT if (tp == 1 and ib == 7) else None)
                    fills.append((pk, cT.v))
                wdma_hist[slot] = cT.v
                w = [(cT, fills[0][1]), (cV, ab_mult_hist)]
                E("scalar", "activation", ffnA[:], fills[0][0][:, 0:1024], AF.Silu,
                  scale=self.s_sb[:, ob:ob + 1], wait=w, inc=cS)
                self.evac_hist.append(cS.v)
                E("scalar", "mul", ffnB[:], fills[1][0][:, 0:1024],
                  self.s_sb[:, 32 + ob:32 + ob + 1], wait=[(cT, fills[1][1])], inc=cS)
                self.evac_hist.append(cS.v)
                E("vector", "tensor_mul", self.A_v[:, ob * 1024:(ob + 1) * 1024],
                  ffnA[:], ffnB[:], wait=[(cS, cS.v)], inc=cV)
                ab_mult_hist = cV.v
            a_done_cv = cV.v

            h_done_cs = 0
            for hname, src_v, dst_v, scol in [("h1", self.A_v, self.B_v, 64),
                                              ("h2", self.B_v, self.A_v, 96)]:
                for ob in range(32):
                    slot = ob % 2
                    E("sync", "dma_start", out=wb(slot)[:],
                      in_=self.wts[hname].ap()[ob * 524288:(ob + 1) * 524288],
                      wait=[(cT, wdma_hist[slot])], inc=dT, incv=16)
                    wdT = dT.v
                    pk, pw = self._psum_fill_wait()
                    pw = pw + [(dT, wdT)]
                    if ob == 0:
                        if hname == "h1":
                            pw.append((cV, a_done_cv))
                        else:
                            pw.append((cS, h_done_cs))
                    for tp in range(2):
                        for ib in range(32):
                            E("tensor", "matmul", pk[:, tp * 512:(tp + 1) * 512],
                              wb(slot)[:, ib * 128:ib * 128 + 128],
                              src_v[:, ib * 1024 + tp * 512:ib * 1024 + tp * 512 + 512],
                              start=(ib == 0), stop=(ib == 31),
                              wait=pw if (tp == 0 and ib == 0) else None,
                              inc=cT if (tp == 1 and ib == 31) else None)
                    wdma_hist[slot] = cT.v
                    E("scalar", "activation", dst_v[:, ob * 1024:(ob + 1) * 1024],
                      pk[:, 0:1024], AF.Gelu, scale=self.s_sb[:, scol + ob:scol + ob + 1],
                      wait=[(cT, cT.v)], inc=cS)
                    self.evac_hist.append(cS.v)
                h_done_cs = cS.v

            h2_mm_ct = cT.v
            for j in range(8):
                loc0 = ch * 1024 + j * 128
                E("scalar", "dma_start", out=self.Bf_v[:, j * 1024:(j + 1) * 1024],
                  in_=self.x2f.ap()[loc0:loc0 + 128, :],
                  wait=[(dG, self.g_x2_written), (cT, h2_mm_ct)] if j == 0 else None,
                  inc=dS, incv=16)
            x2r_dS = dS.v
            w3_region_cv = [cV.v] * 4
            for half in range(2):
                for ib in range(32):
                    slot = ib % 2
                    E("sync", "dma_start", out=wb(slot)[:, 0:1024],
                      in_=self.wts["w3"].ap()[ib * 131072:(ib + 1) * 131072],
                      wait=[(cS, h_done_cs), (cT, wdma_hist[slot])], inc=dT, incv=16)
                    wdT = dT.v
                    for tt in range(4):
                        pk = ps[tt]
                        pw = None
                        if ib == 0:
                            pw = [(dT, wdT), (cS, self.evac_hist[-1])]
                            if half == 1:
                                pw.append((cV, w3_region_cv[tt]))
                        elif tt == 0:
                            pw = [(dT, wdT)]
                        for oc_i in range(2):
                            E("tensor", "matmul", pk[:, oc_i * 512:(oc_i + 1) * 512],
                              self.A_v[:, ib * 1024 + (half * 4 + tt) * 128:
                                       ib * 1024 + (half * 4 + tt) * 128 + 128],
                              wb(slot)[:, oc_i * 512:(oc_i + 1) * 512],
                              start=(ib == 0), stop=(ib == 31),
                              wait=pw if oc_i == 0 else None,
                              inc=cT if (ib == 31 and tt == 3 and oc_i == 1) else None)
                w3_ct = cT.v
                wdma_hist[0] = w3_ct
                wdma_hist[1] = w3_ct
                for tt in range(4):
                    pk = ps[tt]
                    pr = tt % 2
                    R2 = self.tiles[0 + pr]
                    C2 = self.tiles[2 + pr]
                    cX = self.colA if pr == 0 else self.colC
                    cY = self.colB if pr == 0 else self.colD
                    j = half * 4 + tt
                    loc0 = ch * 1024 + j * 128
                    wv = [(cT, w3_ct), (dS, x2r_dS)]
                    if ln2_hist[pr][0] is not None:
                        wv.append((cS, ln2_hist[pr][0]))
                        wv.append((dO, ln2_hist[pr][1]))
                    E("vector", "tensor_mul", R2[:], pk[:, 0:1024],
                      self.w3s_rep[:], wait=wv, inc=cV)
                    E("vector", "tensor_add", R2[:], R2[:],
                      self.Bf_v[:, j * 1024:(j + 1) * 1024], inc=cV)
                    w3_region_cv[tt] = cV.v
                    E("vector", "reduce_sum", cX[:], R2[:], axis=AX.X, inc=cV)
                    E("vector", "tensor_scalar_mul", cX[:], cX[:],
                      -1.0 / 1024, wait=[(cV, cV.v)], inc=cV)
                    neg_cv = cV.v
                    E("scalar", "add", C2[:], R2[:], cX[:, 0:1],
                      wait=[(cV, neg_cv)], inc=cS)
                    c_cs = cS.v
                    E("vector", "tensor_mul", R2[:], C2[:], C2[:],
                      wait=[(cS, c_cs)], inc=cV)
                    E("vector", "reduce_sum", cY[:], R2[:], axis=AX.X, inc=cV)
                    E("vector", "tensor_scalar", cY[:], cY[:],
                      1.0 / 1024, EPS, OP.mult, OP.add, wait=[(cV, cV.v)], inc=cV)
                    var_cv = cV.v
                    E("scalar", "activation", cX[:], cY[:], AF.Sqrt,
                      wait=[(cV, var_cv)], inc=cS)
                    E("vector", "reciprocal", cY[:], cX[:],
                      wait=[(cS, cS.v)], inc=cV)
                    rstd_cv = cV.v
                    E("scalar", "mul", R2[:], C2[:], cY[:, 0:1],
                      wait=[(cV, rstd_cv)], inc=cS)
                    x2_cs = cS.v
                    E("gpsimd", "dma_start", out=self.y[loc0:loc0 + 128, :],
                      in_=R2[:], wait=[(cS, x2_cs)], inc=dO, incv=16)
                    ln2_hist[pr] = (x2_cs, dO.v)
            prev_w3_region = list(w3_region_cv)

    def _finish(self):
        if self.lite & 64:
            return
        self.W("gpsimd", [(self.dG, self.dG.v)])
        self.W("sync", [(self.dT, self.dT.v)])
        self.W("scalar", [(self.dS, self.dS.v), (self.dC, self.c_consts), (self.dO, self.dO.v)])

    def run(self, in_maps, core_ids, **kw):
        with self.st:
            with self.nc.Block() as block:
                @block.tensor
                def _(e):
                    for fn in self.plan.q["tensor"]:
                        fn(e)

                @block.vector
                def _(e):
                    for fn in self.plan.q["vector"]:
                        fn(e)

                @block.scalar
                def _(e):
                    for fn in self.plan.q["scalar"]:
                        fn(e)

                @block.sync
                def _(e):
                    for fn in self.plan.q["sync"]:
                        fn(e)

                @block.gpsimd
                def _(e):
                    for fn in self.plan.q["gpsimd"]:
                        fn(e)
            return run_bass_kernel_spmd(self.nc, in_maps, core_ids=list(core_ids), **kw)



def make_core_inputs(inputs, b, p):
    X = np.asarray(inputs["L0_attn_x"])
    shift = -128 * (1 - p)
    n = np.arange(NQ, dtype=np.float64) - NEG + shift
    nramp = np.tile(np.maximum(n, 0.0).astype(np.float32), (8, 1))
    ind = (n >= 0).astype(np.float32)[None, :]
    pp, qq = np.mgrid[0:128, 0:640]
    m2 = np.where(qq - pp - 256 + shift < 0, np.float32(NEGINF), np.float32(0.0))
    xfull = np.ascontiguousarray(X[b])
    xq = np.ascontiguousarray(X[b].reshape(64, 128, D)[p::2].reshape(TOK, D))
    return {
        "x": xfull, "xq": xq,
        "wqm": np.ascontiguousarray(inputs["L0_attn_Wq"]),
        "wkm": np.ascontiguousarray(inputs["L0_attn_Wk"]),
        "wvm": np.ascontiguousarray(inputs["L0_attn_Wv"]),
        "wom": np.ascontiguousarray(inputs["L0_attn_Wo"]),
        "bo": np.asarray(inputs["L0_attn_bo"], np.float32).reshape(1, D),
        "theta": np.asarray(inputs["L0_attn_theta"], np.float32).reshape(8, 1),
        "gamma": np.ascontiguousarray(np.asarray(inputs["L0_attn_gamma"], np.float32).reshape(8, 128).T),
        "beta": np.ascontiguousarray(np.asarray(inputs["L0_attn_beta"], np.float32).reshape(8, 128).T),
        "w1m": np.ascontiguousarray(inputs["L0_ffn_w1_w"]),
        "s1": np.ascontiguousarray(np.asarray(inputs["L0_ffn_w1_s"], np.float32).reshape(32, 128).T),
        "w2m": np.ascontiguousarray(inputs["L0_ffn_w2_w"]),
        "s2": np.ascontiguousarray(np.asarray(inputs["L0_ffn_w2_s"], np.float32).reshape(32, 128).T),
        "h1m": np.ascontiguousarray(inputs["L0_ffn_h1_w"]),
        "sh1": np.ascontiguousarray(np.asarray(inputs["L0_ffn_h1_s"], np.float32).reshape(32, 128).T),
        "h2m": np.ascontiguousarray(inputs["L0_ffn_h2_w"]),
        "sh2": np.ascontiguousarray(np.asarray(inputs["L0_ffn_h2_s"], np.float32).reshape(32, 128).T),
        "w3m": np.ascontiguousarray(inputs["L0_ffn_w3_w"]),
        "w3s": np.asarray(inputs["L0_ffn_w3_s"], np.float32).reshape(1, D),
        "nramp": nramp, "ind": ind, "m2": m2,
        "o8": np.full((8, 1), 0.125, np.float32),
        "oc": np.ones((1, 128), np.float32),
        "o128f": np.full((128, 1), 1.0 / 128, np.float32),
        "o128b": np.ones((128, 1), ml_dtypes.bfloat16),
    }


def kernel(**inputs):
    b = Builder()
    in_maps = [make_core_inputs(inputs, c // 2, c % 2) for c in range(8)]
    res = b.run(in_maps, core_ids=list(range(8)))
    X = np.asarray(inputs["L0_attn_x"])
    out = np.empty((4, T, D), np.float32)
    for c in range(8):
        bb, p = c // 2, c % 2
        out[bb].reshape(64, 128, D)[p::2] = res.results[c]["y"].reshape(32, 128, D)
    return out


# version 18 (final)
# speedup vs baseline: 1.0575x; 1.0575x over previous
import math
import contextlib
import numpy as np
import ml_dtypes

import concourse.bass as bass
import concourse.mybir as mybir
from concourse.ap import AP
from concourse.bass_utils import run_bass_kernel_spmd

F32 = mybir.dt.float32
BF16 = mybir.dt.bfloat16
AF = mybir.ActivationFunctionType
OP = mybir.AluOpType
AX = mybir.AxisListType

D = 1024
T = 8192
H = 8
HD = 128
TOK = 4096
NG = 4
GT = 1024
NCH = 64
DH = 4096
TC = 1024
NEG = 256
NQ = 8448
T2C = 8576
ISQ = 1.0 / math.sqrt(HD)
EPS = 1e-5
NEGINF = -1e30

ENGINES = ("tensor", "vector", "scalar", "sync", "gpsimd")


class Plan:
    def __init__(self):
        self.q = {e: [] for e in ENGINES}

    def add(self, e, fn):
        self.q[e].append(fn)


class Sem:
    def __init__(self, nc, st, name):
        self.h = st.enter_context(nc.semaphore(name))
        self.v = 0

    def take(self, k=1):
        self.v += k
        return self.v


class Builder:
    def __init__(self, dbg=(), phases=99, lite=0):
        self.lite = lite
        self.dbg = set(dbg)
        self.phases = phases
        self.nc = bass.Bass("TRN2", target_bir_lowering=False)
        self.st = contextlib.ExitStack()
        self.plan = Plan()
        self._build()

    def E(self, e, meth, *a, wait=None, inc=None, incv=1, **k):
        if inc is not None:
            inc.take(incv)
        def fn(eng, meth=meth, a=a, k=k, wait=wait, inc=inc, incv=incv):
            if wait:
                for s, v in wait:
                    eng.wait_ge(s.h, v)
            inst = getattr(eng, meth)(*a, **k)
            if inc is not None:
                inst.then_inc(inc.h, incv)
        self.plan.add(e, fn)

    def W(self, e, waits):
        def fn(eng, waits=waits):
            for s, v in waits:
                eng.wait_ge(s.h, v)
        self.plan.add(e, fn)

    def barrier(self):
        tot = self.bar.take(4)
        ctv = self.cT.v
        for e in ENGINES:
            if e == "tensor":
                def fn(eng, tot=tot):
                    eng.wait_ge(self.bar.h, tot)
                self.plan.add(e, fn)
            else:
                def fn(eng, tot=tot, ctv=ctv):
                    eng.nop().then_inc(self.bar.h, 1)
                    eng.wait_ge(self.bar.h, tot)
                    eng.wait_ge(self.cT.h, ctv)
                self.plan.add(e, fn)

    def _build(self):
        nc, st = self.nc, self.st

        def din(name, shape, dt=F32):
            return nc.dram_tensor(name, shape, dt, kind="ExternalInput").ap()

        def scr(name, shape, dt):
            return nc.dram_tensor(name, shape, dt, kind="Internal")

        self.x = din("x", [T, D])
        self.xq = din("xq", [TOK, D])
        self.Wq = din("wqm", [D, D]); self.Wk = din("wkm", [D, D])
        self.Wv = din("wvm", [D, D]); self.Wo = din("wom", [D, D])
        self.bo = din("bo", [1, D])
        self.theta = din("theta", [8, 1])
        self.gamma = din("gamma", [128, 8]); self.beta = din("beta", [128, 8])
        self.w1 = din("w1m", [DH, D]); self.w2 = din("w2m", [DH, D])
        self.h1 = din("h1m", [DH, DH]); self.h2 = din("h2m", [DH, DH])
        self.w3 = din("w3m", [D, DH])
        self.s1 = din("s1", [128, 32]); self.s2 = din("s2", [128, 32])
        self.sh1 = din("sh1", [128, 32]); self.sh2 = din("sh2", [128, 32])
        self.w3s = din("w3s", [1, D])
        self.nramp = din("nramp", [8, NQ])
        self.ind = din("ind", [1, NQ])
        self.m2 = din("m2", [128, 640])
        self.o8 = din("o8", [8, 1])
        self.oc = din("oc", [1, 128])
        self.o128f = din("o128f", [128, 1])
        self.o128b = din("o128b", [128, 1], BF16)
        self.y = nc.dram_tensor("y", [TOK, D], F32, kind="ExternalOutput").ap()

        self.xb = scr("xb", [T, D], BF16)
        self.xqb = scr("xqb", [TOK, D], BF16)
        self.dflat = scr("dflat", [NQ], F32)
        self.t2d = scr("t2d", [128 * T2C], F32)
        self.ktS = scr("ktS", [NCH * 128 * 1024], BF16)
        self.vS = scr("vS", [NCH * 128 * 1024], BF16)
        self.qS = scr("qS", [NG * 128 * 8192], BF16)
        self.x2f = scr("x2f", [TOK, D], F32)
        self.x2b = scr("x2b", [TOK, D], BF16)
        self.wbs = {
            "w1": scr("wb1", [DH, D], BF16), "w2": scr("wb2", [DH, D], BF16),
            "h1": scr("wbh1", [DH, DH], BF16), "h2": scr("wbh2", [DH, DH], BF16),
            "w3": scr("wb3", [D, DH], BF16),
        }
        self.wts = {
            "w1": scr("w1t", [32 * 128 * 8 * 128], BF16),
            "w2": scr("w2t", [32 * 128 * 8 * 128], BF16),
            "h1": scr("h1t", [32 * 128 * 32 * 128], BF16),
            "h2": scr("h2t", [32 * 128 * 32 * 128], BF16),
            "w3": scr("w3t", [32 * 128 * 1024], BF16),
        }

        self.dbg_outs = {}
        shapes = {
            "dbg_kt": ([NCH * 128 * 1024], BF16),
            "dbg_v": ([NCH * 128 * 1024], BF16),
            "dbg_q": ([NG * 128 * 8192], BF16),
            "dbg_t2": ([128 * T2C], F32),
            "dbg_o": ([128, 8192], F32),
            "dbg_l": ([1, 8 * GT], F32),
            "dbg_g": ([128, 8192], BF16),
            "dbg_x2": ([TOK, D], F32),
            "dbg_w1t": ([32 * 128 * 8 * 128], BF16),
            "dbg_a": ([128, 8192], F32),
        }
        for name in self.dbg:
            shp, dt = shapes[name]
            self.dbg_outs[name] = nc.dram_tensor(name, shp, dt, kind="ExternalOutput").ap()

        sbuf = lambda name, shape, dt: st.enter_context(nc.sbuf_tensor(name, shape, dt))

        AR_COLS = 73984
        self.arena = sbuf("arena", [128, AR_COLS], BF16)
        arena_f = self.arena.bitcast(F32)

        def cb(c0, n):
            return self.arena[:, c0:c0 + n]

        def cf(c0, n):
            return arena_f[:, c0 // 2:c0 // 2 + n]

        self.T2f = cf(0, T2C)
        self.ind_sb = cf(0, NQ)[0:1, :]
        self.NRf = cf(20736, NQ)[0:8, :]
        self.Dtab_sb = cf(20736, NQ)[0:1, :]
        self.EXf = cf(41216, NQ)[0:8, :]
        self.kstage = [cb(17152, 512), cb(17664, 512)]
        self.vstage = [cb(18176, 1024), cb(19200, 1024)]
        self.lcarv = [cf(17152, 1024), cf(19200, 1024), cf(21248, 1024)]
        self.XTc = [cb(24832, 8192), cb(33024, 8192)]
        self.Wqkv = cb(41216, 16384)
        self.wtstage = [cb(57600, 4096), cb(61696, 4096)]
        self.O_sb = cf(24832, 8192)
        self.Qg_v = cb(41216, 8192)
        self.Kbuf = [cb(49408, 4096), cb(53504, 4096)]
        self.Vbuf = [cb(57600, 4096), cb(61696, 4096)]
        self.GT_v = cb(65792, 8192)
        self.A_v = cb(0, 32768)
        self.B_v = cb(32768, 32768)
        self.Bf_v = cf(32768, 16384)
        self.X2T_v = cb(65536, 8192)

        self.Wo_sb = sbuf("wo_sb", [128, 8192], BF16)
        self.M2sb = sbuf("m2sb", [128, 640], F32)
        self.th_sb = sbuf("th_sb", [8, 1], F32)
        self.nth_sb = sbuf("nth_sb", [8, 1], F32)
        self.o8_sb = sbuf("o8_sb", [8, 1], F32)
        self.oc_sb = sbuf("oc_sb", [1, 128], F32)
        self.o128f_sb = sbuf("o128f_sb", [128, 1], F32)
        self.o128b_sb = sbuf("o128b_sb", [128, 1], BF16)
        self.gm_sb = sbuf("gm_sb", [128, 8], F32)
        self.bt_sb = sbuf("bt_sb", [128, 8], F32)
        self.s_sb = sbuf("s_sb", [128, 128], F32)
        self.bo_rep = sbuf("bo_rep", [128, D], F32)
        self.w3s_rep = sbuf("w3s_rep", [128, D], F32)
        self.bo_row = self.bo_rep[0:1, :]
        self.w3s_row = self.w3s_rep[0:1, :]
        self.SD = [sbuf("sd0", [128, 1024], F32), sbuf("sd1", [128, 1024], F32)]
        self.Pb = [sbuf("pb0", [128, 1024], BF16), sbuf("pb1", [128, 1024], BF16)]
        self.tiles = [sbuf(f"tile{i}", [128, 1024], F32) for i in range(4)]
        self.x2bt = sbuf("x2bt", [128, 1024], BF16)
        self.colA = sbuf("colA", [128, 1], F32)
        self.colB = sbuf("colB", [128, 1], F32)
        self.colC = sbuf("colC", [128, 1], F32)
        self.colD = sbuf("colD", [128, 1], F32)
        self.wbuf = [self.Wo_sb[:, 0:4096], self.Wo_sb[:, 4096:8192]]

        self.ps = [st.enter_context(nc.psum_tensor(f"ps{i}", [128, 1024], F32)) for i in range(4)]

        s = lambda name: Sem(nc, st, name)
        self.dC = s("dC"); self.dG = s("dG"); self.dT = s("dT"); self.dS = s("dS")
        self.dW = s("dW"); self.dO = s("dO")
        self.cT = s("cT"); self.cV = s("cV"); self.cS = s("cS")
        self.vsd = s("vsd"); self.spp = s("spp"); self.tpl = s("tpl")
        self.tS = s("tS"); self.gdn = s("gdn")
        self.vev = s("vev"); self.tblk = s("tblk"); self.bar = s("bar")
        self.tgrp = s("tgrp")

        self.fill_k = 0
        self.evac_hist = []

        self._ph0()
        if not (self.lite & 32):
            self.barrier()
        if self.phases >= 1:
            self._ph1()
        if self.phases >= 2:
            self._ph25()
            self._ph2()
            self.barrier()
        if self.phases >= 3:
            self._ph3()
            self.barrier()
        if self.phases >= 4:
            self._ph4()
        if self.phases >= 5:
            self._ph5()
        self._finish()

    def _ph0(self):
        E, W = self.E, self.W
        dC, dG, dS, cT, cV, cS = self.dC, self.dG, self.dS, self.cT, self.cV, self.cS
        E("scalar", "dma_start", out=self.th_sb[:], in_=self.theta, inc=dC, incv=16)
        th_done = dC.v
        const_list = [] if (self.lite & 1) else None
        for dst, src in ([(self.o8_sb[:], self.o8)] if (self.lite & 1) else
                        [(self.M2sb[:], self.m2), (self.o8_sb[:], self.o8),
                         (self.oc_sb[:], self.oc), (self.o128f_sb[:], self.o128f),
                         (self.o128b_sb[:], self.o128b), (self.bo_row[:], self.bo),
                         (self.w3s_row[:], self.w3s)]) + [(self.ind_sb[:], self.ind)]:
            E("scalar", "dma_start", out=dst, in_=src, inc=dC, incv=16)
        if not (self.lite & 1):
            E("scalar", "dma_start", out=self.gm_sb[:], in_=self.gamma, inc=dC, incv=16)
            E("scalar", "dma_start", out=self.bt_sb[:], in_=self.beta, inc=dC, incv=16)
            for i, t in enumerate([self.s1, self.s2, self.sh1, self.sh2]):
                E("scalar", "dma_start", out=self.s_sb[:, i * 32:(i + 1) * 32],
                  in_=t, inc=dC, incv=16)
        self.c_consts = dC.v
        E("gpsimd", "dma_start", out=self.NRf, in_=self.nramp, inc=dG, incv=16)
        g_nr = dG.v

        if not (self.lite & 4):
            E("vector", "tensor_scalar_mul", self.nth_sb[:], self.th_sb[:], -1.0,
              wait=[(dC, th_done)], inc=cV)
            v_nth = cV.v
            E("scalar", "activation", self.EXf, self.NRf, AF.Exp, scale=self.nth_sb[0:8, 0:1],
              wait=[(dG, g_nr), (cV, v_nth)], inc=cS)
        s_expn = cS.v
        mul_hist = []
        pieces = [] if (self.lite & 8) else [(k * 512, 512) for k in range(NQ // 512)]
        if NQ % 512 and not (self.lite & 8):
            pieces.append((NQ - NQ % 512, NQ % 512))
        for k, (c0, wdt) in enumerate(pieces):
            pk = self.ps[k % 2]
            w = [(cS, s_expn), (dC, self.c_consts)] if k < 2 else [(cV, mul_hist[k - 2])]
            E("tensor", "matmul", pk[0:1, 0:wdt], self.o8_sb[:],
              self.EXf[:, c0:c0 + wdt], start=True, stop=True, wait=w, inc=cT)
            E("vector", "tensor_mul", self.Dtab_sb[:, c0:c0 + wdt],
              pk[0:1, 0:wdt], self.ind_sb[:, c0:c0 + wdt],
              wait=[(cT, cT.v)], inc=cV)
            mul_hist.append(cV.v)
        if self.lite & 16:
            self.s_t2sb = dS.v
            return
        E("gpsimd", "dma_start", out=self.dflat.ap(), in_=self.Dtab_sb,
          wait=[(cV, cV.v)], inc=dG, incv=16)
        E("gpsimd", "dma_start",
          out=AP(self.t2d.ap().tensor, 0, [[T2C + 1, 128], [1, NQ]]),
          in_=AP(self.dflat.ap().tensor, 0, [[0, 128], [1, NQ]]),
          wait=[(dG, dG.v)], inc=dG, incv=16)
        g_t2d = dG.v
        E("scalar", "dma_start", out=self.T2f,
          in_=AP(self.t2d.ap().tensor, 0, [[T2C, 128], [1, T2C]]),
          wait=[(dG, g_t2d)], inc=dS, incv=16)
        self.s_t2sb = dS.v
        if "dbg_t2" in self.dbg_outs:
            E("gpsimd", "dma_start", out=self.dbg_outs["dbg_t2"], in_=self.t2d.ap(),
              wait=[(dG, g_t2d)], inc=dG, incv=16)

        for rep, row, pp in ([] if (self.lite & 2) else [(self.bo_rep, self.bo_row, 2), (self.w3s_rep, self.w3s_row, 3)]):
            for half in range(2):
                E("tensor", "matmul", self.ps[pp][:, half * 512:(half + 1) * 512],
                  self.oc_sb[:], row[0:1, half * 512:(half + 1) * 512],
                  start=True, stop=True, wait=[(dC, self.c_consts)], inc=cT)
                E("vector", "tensor_copy", rep[:, half * 512:(half + 1) * 512],
                  self.ps[pp][:, half * 512:(half + 1) * 512], wait=[(cT, cT.v)], inc=cV)
        if "dbg_o" in self.dbg_outs and self.phases < 1:
            E("gpsimd", "dma_start", out=self.dbg_outs["dbg_o"][:, 0:1024],
              in_=self.bo_rep[:], wait=[(cV, cV.v)], inc=dG, incv=16)
            E("gpsimd", "dma_start", out=self.dbg_outs["dbg_o"][:, 1024:2048],
              in_=self.w3s_rep[:], wait=[(cV, cV.v)], inc=dG, incv=16)

    def _ph1(self):
        E = self.E
        dG = self.dG
        self.g_xb_piece = []
        for i in range(8):
            E("gpsimd", "dma_start", out=self.xb.ap()[i * 1024:(i + 1) * 1024, :],
              in_=self.x[i * 1024:(i + 1) * 1024, :], inc=dG, incv=16)
            self.g_xb_piece.append(dG.v)
        self.g_xb = dG.v
        for i in range(4):
            E("gpsimd", "dma_start", out=self.xqb.ap()[i * 1024:(i + 1) * 1024, :],
              in_=self.xq[i * 1024:(i + 1) * 1024, :], inc=dG, incv=16)
        self.g_xqb = dG.v
        for wsrc, c0 in [(self.Wk, 0), (self.Wv, 8192)]:
            E("gpsimd", "dma_start", out=self.Wqkv[:, c0:c0 + 8192],
              in_=AP(wsrc.tensor, 0, [[1024, 128], [131072, 8], [1, 1024]]), inc=dG, incv=16)
        self.g_wkv = dG.v
        E("gpsimd", "dma_start", out=self.Wo_sb[:],
          in_=AP(self.Wo.tensor, 0, [[1024, 128], [131072, 8], [1, 1024]]), inc=dG, incv=16)
        self.g_wo = dG.v
        if "dbg_o" in self.dbg_outs and self.phases < 2:
            E("gpsimd", "dma_start", out=self.dbg_outs["dbg_o"],
              in_=self.Wo_sb[:], wait=[(dG, self.g_wo)], inc=dG, incv=16)

    def _psum_fill_wait(self):
        k = self.fill_k
        self.fill_k += 1
        if k >= 4:
            return self.ps[k % 4], [(self.cS, self.evac_hist[k - 4])]
        return self.ps[k % 4], []

    def _proj_pass(self, nchunks, src_dram, wcol0, scale, dst_writer, tdone_list):
        E = self.E
        dT, dG, cT, cS = self.dT, self.dG, self.cT, self.cS
        stage_slot_dma = [0, 0]
        for tc in range(nchunks):
            self._pump_prep(25)
            buf = self.XTc[tc % 2]
            w = [(dG, self.g_stream_ready)]
            if tc >= 2:
                w.append((cT, tdone_list[tc - 2]))
            for ib in range(8):
                E("sync", "dma_start", out=buf[:, ib * 1024:(ib + 1) * 1024],
                  in_=src_dram.ap()[tc * 1024:(tc + 1) * 1024, ib * 128:(ib + 1) * 128],
                  transpose=True, wait=w if ib == 0 else None, inc=dT, incv=16)
            tr_done = dT.v
            for o in range(8):
                for tp in range(2):
                    pk, pw = self._psum_fill_wait()
                    pw = pw + [(dT, tr_done)]
                    for i in range(8):
                        E("tensor", "matmul", pk[:, 0:512],
                          self.Wqkv[:, wcol0 + i * 1024 + o * 128:wcol0 + i * 1024 + o * 128 + 128],
                          buf[:, i * 1024 + tp * 512:i * 1024 + tp * 512 + 512],
                          start=(i == 0), stop=(i == 7),
                          wait=pw if i == 0 else None, inc=cT if i == 7 else None)
                    slot = (o * 2 + tp) % 2
                    ew = [(cT, cT.v)]
                    if stage_slot_dma[slot]:
                        ew.append((self.dS, stage_slot_dma[slot]))
                    if scale is None:
                        E("scalar", "copy", self.kstage[slot][:], pk[:, 0:512], wait=ew, inc=cS)
                    else:
                        E("scalar", "activation", self.kstage[slot][:], pk[:, 0:512],
                          AF.Copy, scale=scale, wait=ew, inc=cS)
                    self.evac_hist.append(cS.v)
                    dst_writer(tc, o, tp, self.kstage[slot])
                    stage_slot_dma[slot] = self.dS.v
            tdone_list.append(cT.v)

    def _ph2(self):
        E = self.E
        dG, cT, cS = self.dG, self.cT, self.cS
        self.g_stream_ready = self.g_xb
        tdone = []

        def write_k(tc, o, tp, stage):
            cbase = tc * 8 + tp * 4
            E("gpsimd", "dma_start",
              out=AP(self.ktS.ap().tensor, cbase * 131072 + o * 128,
                     [[1024, 128], [131072, 4], [1, 128]]),
              in_=stage[:], wait=[(cS, cS.v)], inc=dG, incv=16)

        dT = self.dT
        kslot_dma = [0, 0]
        vslot_dma = [0, 0]
        for tc in range(8):
            self._pump_prep(25)
            buf = self.XTc[tc % 2]
            w = [(dG, self.g_xb_piece[tc]), (dG, self.g_wkv)]
            if tc >= 2:
                w.append((cT, tdone[tc - 2]))
            for ib in range(8):
                E("sync", "dma_start", out=buf[:, ib * 1024:(ib + 1) * 1024],
                  in_=self.xb.ap()[tc * 1024:(tc + 1) * 1024, ib * 128:(ib + 1) * 128],
                  transpose=True, wait=w if ib == 0 else None, inc=dT, incv=16)
            tr_done = dT.v
            for o in range(8):
                for tp in range(2):
                    pk, pw = self._psum_fill_wait()
                    pw = pw + [(dT, tr_done)]
                    for i in range(8):
                        E("tensor", "matmul", pk[:, 0:512],
                          self.Wqkv[:, i * 1024 + o * 128:i * 1024 + o * 128 + 128],
                          buf[:, i * 1024 + tp * 512:i * 1024 + tp * 512 + 512],
                          start=(i == 0), stop=(i == 7),
                          wait=pw if i == 0 else None, inc=cT if i == 7 else None)
                    slot = tp
                    ew = [(cT, cT.v)]
                    if kslot_dma[slot]:
                        ew.append((self.dS, kslot_dma[slot]))
                    E("scalar", "copy", self.kstage[slot][:], pk[:, 0:512], wait=ew, inc=cS)
                    self.evac_hist.append(cS.v)
                    cbase = tc * 8 + tp * 4
                    E("scalar", "dma_start",
                      out=AP(self.ktS.ap().tensor, cbase * 131072 + o * 128,
                             [[1024, 128], [131072, 4], [1, 128]]),
                      in_=self.kstage[slot][:], inc=self.dS, incv=16)
                    kslot_dma[slot] = self.dS.v
            for tt in range(8):
                pk, pw = self._psum_fill_wait()
                for oc_i in range(2):
                    for i in range(8):
                        E("tensor", "matmul", pk[:, oc_i * 512:(oc_i + 1) * 512],
                          buf[:, i * 1024 + tt * 128:i * 1024 + tt * 128 + 128],
                          self.Wqkv[:, 8192 + i * 1024 + oc_i * 512:8192 + i * 1024 + oc_i * 512 + 512],
                          start=(i == 0), stop=(i == 7),
                          wait=pw if (i == 0 and oc_i == 0) else None,
                          inc=cT if (i == 7 and oc_i == 1) else None)
                slot = tt % 2
                ew = [(cT, cT.v)]
                if vslot_dma[slot]:
                    ew.append((self.dS, vslot_dma[slot]))
                E("scalar", "copy", self.vstage[slot][:], pk[:], wait=ew, inc=cS)
                self.evac_hist.append(cS.v)
                cix = tc * 8 + tt
                E("scalar", "dma_start", out=self.vS.ap()[cix * 131072:(cix + 1) * 131072],
                  in_=self.vstage[slot][:], inc=self.dS, incv=16)
                vslot_dma[slot] = self.dS.v
            tdone.append(cT.v)
        self.g_kv_written = None
        self.g_kv_written_dS = self.dS.v
        if "dbg_kt" in self.dbg_outs:
            E("gpsimd", "dma_start", out=self.dbg_outs["dbg_kt"], in_=self.ktS.ap(),
              wait=[(dG, self.g_kv_written)], inc=dG, incv=16)
        if "dbg_v" in self.dbg_outs:
            E("gpsimd", "dma_start", out=self.dbg_outs["dbg_v"], in_=self.vS.ap(),
              wait=[(dG, self.g_kv_written)], inc=dG, incv=16)

    def _ph3(self):
        E = self.E
        dG, cS = self.dG, self.cS
        E("gpsimd", "dma_start", out=self.Wqkv[:, 0:8192],
          in_=AP(self.Wq.tensor, 0, [[1024, 128], [131072, 8], [1, 1024]]), inc=dG, incv=16)
        self.g_stream_ready = dG.v
        tdone = []

        def write_q(tc, o, tp, stage):
            E("scalar", "dma_start",
              out=AP(self.qS.ap().tensor, tc * 1048576 + o * 1024 + tp * 512,
                     [[8192, 128], [1, 512]]),
              in_=stage[:], inc=self.dS, incv=16)

        self._proj_pass(4, self.xqb, 0, ISQ, write_q, tdone)
        self._prep_finalize()
        self.g_q_written_dS = self.dS.v
        if "dbg_q" in self.dbg_outs:
            E("gpsimd", "dma_start", out=self.dbg_outs["dbg_q"], in_=self.qS.ap(),
              wait=[(dG, self.g_q_written)], inc=dG, incv=16)



    def _ph25(self):
        E = self.E
        dG = self.dG
        dT = self.dT
        dS = self.dS
        items = []
        cast_done = {}

        def mk_cast(nm, src_t, r0, r1):
            def f():
                E("gpsimd", "dma_start", out=self.wbs[nm].ap()[r0:r1, :],
                  in_=src_t[r0:r1, :], inc=dG, incv=16)
                cast_done[nm] = dG.v
            return f

        self._stage_dma = [0, 0]
        self._prep_k = 0

        def mk_tr(nm, ib, nib, nob, obstride, rows):
            def f():
                slot = self._prep_k % 2
                st_v = self.wtstage[slot][:, 0:rows]
                w = [(dG, cast_done[nm])]
                if self._stage_dma[slot]:
                    w.append((dS, self._stage_dma[slot]))
                E("sync", "dma_start", out=st_v,
                  in_=self.wbs[nm].ap()[:, ib * 128:(ib + 1) * 128],
                  transpose=True, wait=w, inc=dT, incv=16)
                tr = dT.v
                if nm == "w3":
                    E("scalar", "dma_start",
                      out=self.wts[nm].ap()[ib * 131072:(ib + 1) * 131072],
                      in_=st_v, wait=[(dT, tr)], inc=dS, incv=16)
                else:
                    E("scalar", "dma_start",
                      out=AP(self.wts[nm].ap().tensor, ib * 128,
                             [[nib * 128, 128], [obstride, nob], [1, 128]]),
                      in_=st_v, wait=[(dT, tr)], inc=dS, incv=16)
                self._stage_dma[slot] = dS.v
                self._prep_k += 1
            return f

        cfg = {"w1": (self.w1, DH, D, 8, 32, 131072), "w2": (self.w2, DH, D, 8, 32, 131072),
               "h1": (self.h1, DH, DH, 32, 32, 524288), "h2": (self.h2, DH, DH, 32, 32, 524288),
               "w3": (self.w3, D, DH, 32, 0, 0)}
        for nm in ["w1", "w2", "h1", "h2", "w3"]:
            src_t, rows, cols, nib, nob, obstride = cfg[nm]
            npiece = max(1, (rows * cols * 4) // (8 << 20))
            rp = rows // npiece
            for k in range(npiece):
                items.append(mk_cast(nm, src_t, k * rp, (k + 1) * rp))
            for ib in range(nib):
                items.append(mk_tr(nm, ib, nib, nob, obstride, rows))
        self._prep_items = items

    def _pump_prep(self, n):
        while n > 0 and self._prep_items:
            self._prep_items.pop(0)()
            n -= 1

    def _prep_finalize(self):
        while self._prep_items:
            self._prep_items.pop(0)()
        self.prep_dG = self.dG.v
        self.prep_dT = self.dT.v
        self.prep_dS = self.dS.v

    def _flush_pv(self, pending, tpl_hist, spp_hist):
        E = self.E
        d = pending.pop(0)
        kslot = d["kk"] % 4
        pbb = self.Pb[kslot // 2][:, (kslot % 2) * 512:(kslot % 2) * 512 + 512]
        w = [(self.spp, spp_hist[d["kk"]])]
        if d["first"]:
            w.append((self.vev, self.vev.v))
            w.append((self.dS, d["vdma"]))
        E("tensor", "matmul", self.ps[2][:, d["p0"]:d["p1"]],
          self.Vbuf[d["gb"] % 2][:, d["ci"] * 1024 + d["h"] * 128:d["ci"] * 1024 + d["h"] * 128 + 128],
          pbb[:, 0:d["p1"] - d["p0"]],
          start=(d["ci"] == 0), stop=(d["ci"] == 3), wait=w)
        row = 0 if d["p0"] < 512 else 32
        E("tensor", "matmul", self.ps[3][row:row + 1, d["p0"]:d["p1"]],
          self.o128b_sb[:], pbb[:, 0:d["p1"] - d["p0"]],
          start=(d["ci"] == 0), stop=(d["ci"] == 3),
          tile_position=(0, row), inc=self.tpl)
        tpl_hist.append(self.tpl.v)


    def _flush_evac(self, evac_pending):
        E = self.E
        d = evac_pending.pop(0)
        h, b, vs0 = d["h"], d["b"], d["vs0"]
        w = [(self.tpl, d["tplv"])]
        if b == 0:
            E("vector", "tensor_copy", self.O_sb[:, h * 1024:(h + 1) * 1024],
              self.ps[2][:, 0:1024], wait=w, inc=self.vev)
            E("vector", "tensor_copy", self.lrow(h), self.ps[3][0:1, 0:1024], inc=self.vev)
        else:
            E("vector", "tensor_add",
              self.O_sb[:, h * 1024 + vs0:(h + 1) * 1024],
              self.O_sb[:, h * 1024 + vs0:(h + 1) * 1024],
              self.ps[2][:, vs0:1024], wait=w, inc=self.vev)
            E("vector", "tensor_add", self.lrow(h)[:, vs0:],
              self.lrow(h)[:, vs0:], self.ps[3][0:1, vs0:1024], inc=self.vev)

    def lrow(self, h):
        return self.lcarv[h // 3][32 * (h % 3):32 * (h % 3) + 1, :]


    def _ph4(self):
        E = self.E
        dS, dG, cS, cV, cT = self.dS, self.dG, self.cS, self.cV, self.cT
        self.W("vector", [(dS, self.g_q_written_dS)])
        tS, vsd, spp, tpl, vev = self.tS, self.vsd, self.spp, self.tpl, self.vev
        ps = self.ps
        smult_hist = []
        sfill_hist = []
        svsd_full = []
        spp_hist = []
        tpl_hist = []
        kk = 0
        blk_tpl = []
        kv_dma_hist = []
        kd_hist = []
        grp_tpl = []

        last_gT_cs = 0

        pending = []
        evac_pending = []
        for g in range(NG):
            nch = 16 * g + 16
            nblk = nch // 4
            w = [(dS, self.g_q_written_dS)]
            if g > 0:
                w.append((tpl, grp_tpl[g - 1]))
            E("scalar", "dma_start", out=self.Qg_v[:],
              in_=self.qS.ap()[g * 1048576:(g + 1) * 1048576],
              wait=w, inc=dS, incv=16)
            qg_dma = dS.v

            for b in range(nblk):
                c0 = b * 4
                gb = len(blk_tpl)
                w = [(dS, self.g_kv_written_dS)]
                if gb >= 2:
                    w.append((tpl, blk_tpl[gb - 2]))
                E("scalar", "dma_start", out=self.Kbuf[gb % 2][:],
                  in_=AP(self.ktS.ap().tensor, c0 * 131072,
                         [[1024, 128], [131072, 4], [1, 1024]]),
                  wait=w, inc=dS, incv=16)
                kd_hist.append(dS.v)
                wv_ = None
                if gb == 0:
                    wv_ = [(dG, self.prep_dG), (self.dT, self.prep_dT), (dS, self.prep_dS)]
                E("scalar", "dma_start", out=self.Vbuf[gb % 2][:],
                  in_=AP(self.vS.ap().tensor, c0 * 131072,
                         [[1024, 128], [131072, 4], [1, 1024]]),
                  wait=wv_, inc=dS, incv=16)
                kv_dma_hist.append(dS.v)

                for h in range(8):
                    npieces_head = 0
                    first_cols = None
                    for ci in range(4):
                        c = c0 + ci
                        jmin = max(0, (c - 16 * g) // 2)
                        vs = jmin * 128
                        jm = (c - 16 * g) // 2
                        q0m = 384 if (c - 16 * g) % 2 == 0 else 256
                        pieces = []
                        if vs < 512:
                            pieces.append((vs, 512))
                        pieces.append((max(512, vs), 1024))
                        if first_cols is None:
                            first_cols = pieces[0][0]
                        for (p0, p1) in pieces:
                            kslot = kk % 4
                            sp = self.ps[kslot // 2][:, (kslot % 2) * 512:(kslot % 2) * 512 + 512]
                            sdb = self.SD[kslot // 2][:, (kslot % 2) * 512:(kslot % 2) * 512 + 512]
                            pbb = self.Pb[kslot // 2][:, (kslot % 2) * 512:(kslot % 2) * 512 + 512]
                            w = [(dS, kd_hist[gb]), (dS, qg_dma)]
                            if kk >= 4:
                                w.append((vsd, smult_hist[kk - 4]))
                            if b == 0 and h == 0 and ci == 0 and g > 0 and p0 == pieces[0][0]:
                                w.append((cV, grp_vec_done))
                            E("tensor", "matmul", sp[:, 0:p1 - p0],
                              self.Kbuf[gb % 2][:, ci * 1024 + h * 128:ci * 1024 + h * 128 + 128],
                              self.Qg_v[:, h * 1024 + p0:h * 1024 + p1],
                              start=True, stop=True, wait=w, inc=tS)
                            sfill_hist.append(tS.v)
                            q0 = (16 * g + 1 - c) * 128 + NEG
                            j0 = p0 // 128
                            nj = (p1 - p0) // 128
                            t2v = self.T2f[:, q0 + j0 * 256:q0 + j0 * 256 + nj * 256]
                            t2v = t2v.rearrange("p (j f) -> p j f", f=256)[:, :, 0:128]
                            sview = sp[:, 0:p1 - p0].rearrange("p (j f) -> p j f", f=128)
                            w = [(tS, sfill_hist[kk])]
                            if kk >= 4:
                                w.append((spp, spp_hist[kk - 4]))
                            E("vector", "tensor_mul",
                              sdb[:, 0:p1 - p0].rearrange("p (j f) -> p j f", f=128),
                              sview, t2v, wait=w, inc=vsd)
                            if p0 <= jm * 128 < p1:
                                E("vector", "tensor_add",
                                  sdb[:, jm * 128 - p0:jm * 128 - p0 + 128],
                                  sdb[:, jm * 128 - p0:jm * 128 - p0 + 128],
                                  self.M2sb[:, q0m:q0m + 128], inc=vsd)
                            smult_hist.append(vsd.v)
                            w = [(vsd, smult_hist[kk])]
                            if kk >= 4:
                                w.append((tpl, tpl_hist[kk - 4]))
                            E("scalar", "activation", pbb[:, 0:p1 - p0], sdb[:, 0:p1 - p0],
                              AF.Exp, wait=w, inc=spp)
                            spp_hist.append(spp.v)
                            pending.append(dict(gb=gb, ci=ci, p0=p0, p1=p1, h=h,
                                                kk=kk, vdma=kv_dma_hist[gb],
                                                first=(ci == 0 and p0 == first_cols)))
                            while len(pending) > 3:
                                self._flush_pv(pending, tpl_hist, spp_hist)
                            kk += 1
                            npieces_head += 1
                            if npieces_head == 2 and evac_pending:
                                self._flush_evac(evac_pending)
                    vs0 = max(0, (c0 - 16 * g) // 2) * 128
                    evac_pending.append(dict(h=h, b=b, vs0=vs0,
                                             tplv=tpl.v + len(pending)))
                blk_tpl.append(tpl.v + len(pending))
            while pending:
                self._flush_pv(pending, tpl_hist, spp_hist)
            while evac_pending:
                self._flush_evac(evac_pending)
            grp_tpl.append(tpl.v)
            grp_vev = vev.v

            if "dbg_o" in self.dbg_outs and g == 0:
                E("gpsimd", "dma_start", out=self.dbg_outs["dbg_o"],
                  in_=self.O_sb, wait=[(vev, grp_vev)], inc=dG, incv=16)
                for h in range(8):
                    E("gpsimd", "dma_start",
                      out=self.dbg_outs["dbg_l"][:, h * 1024:(h + 1) * 1024],
                      in_=self.lrow(h), wait=[(vev, grp_vev)], inc=dG, incv=16)

            stats_free = [None, None]
            bc_free = [None, None]
            gt_hist = []
            sq_big_cs = {}
            for h in range(8):
                pr = h % 2
                rA = self.SD[pr][0:1, :]
                rB = self.tiles[pr][0:1, :]
                rC = self.tiles[2 + pr][0:1, :]
                sqt = self.tiles[2 + pr]
                hc = slice(h * 1024, (h + 1) * 1024)
                w = [(vev, grp_vev)]
                if h >= 2:
                    w.append((cT, sq_free[pr]))
                if h >= 2:
                    w.append((cV, gt_hist[h - 2]))
                E("scalar", "activation", sqt[:], self.O_sb[:, hc], AF.Square,
                  wait=w, inc=cS)
                sq_big_cs[h] = cS.v
                w = [(vev, grp_vev), (cS, sq_big_cs[h])]
                if h >= 1 and stats_free[0] is not None:
                    w.append((cS, stats_free[0]))
                    w.append((cV, stats_free[1]))
                if h == 0:
                    w.append((vsd, smult_hist[kk - 1]))
                for pi, (p0, p1) in enumerate([(0, 512), (512, 1024)]):
                    E("tensor", "matmul", ps[0][0:1, p0:p1], self.o128f_sb[:],
                      self.O_sb[:, h * 1024 + p0:h * 1024 + p1],
                      start=True, stop=True, wait=w if pi == 0 else None)
                for pi, (p0, p1) in enumerate([(0, 512), (512, 1024)]):
                    E("tensor", "matmul", ps[1][0:1, p0:p1], self.o128f_sb[:],
                      sqt[:, p0:p1], start=True, stop=True,
                      inc=cT if pi == 1 else None)
                stats_ct = cT.v
                if h >= 2:
                    pass
                sq_free_prev = sq_free[pr] if h >= 2 else None
                E("scalar", "copy", rC, ps[0][0:1, :], wait=[(cT, stats_ct)], inc=cS)
                E("scalar", "activation", rA, ps[0][0:1, :], AF.Square, inc=cS)
                sq_cs = cS.v
                E("vector", "tensor_sub", rA, ps[1][0:1, :], rA,
                  wait=[(cS, sq_cs)], inc=cV)
                var_cv = cV.v
                stats_free = [sq_cs, var_cv]
                E("scalar", "activation", rB, self.lrow(h), AF.Square, inc=cS)
                lsq_cs = cS.v
                E("vector", "scalar_tensor_tensor", rB, rB, EPS, rA,
                  OP.mult, OP.add, wait=[(cS, lsq_cs), (cV, var_cv)], inc=cV)
                den_cv = cV.v
                E("scalar", "activation", rA, rB, AF.Ln, wait=[(cV, den_cv)], inc=cS)
                E("scalar", "activation", rB, rA, AF.Exp, scale=-0.5, inc=cS)
                rstd_cs = cS.v
                w = [(cS, sq_cs)]
                if bc_free[0] is not None:
                    w.append((cV, bc_free[0]))
                for pi, (p0, p1) in enumerate([(0, 512), (512, 1024)]):
                    E("tensor", "matmul", ps[2][:, p0:p1], self.oc_sb[:],
                      rC[:, p0:p1], start=True, stop=True,
                      wait=w if pi == 0 else None)
                w2 = [(cS, rstd_cs)]
                if bc_free[1] is not None:
                    w2.append((cV, bc_free[1]))
                for pi, (p0, p1) in enumerate([(0, 512), (512, 1024)]):
                    E("tensor", "matmul", ps[3][:, p0:p1], self.oc_sb[:],
                      rB[:, p0:p1], start=True, stop=True,
                      wait=w2 if pi == 0 else None, inc=cT if pi == 1 else None)
                bc_ct = cT.v
                E("vector", "tensor_sub", sqt[:], self.O_sb[:, hc],
                  ps[2][:, 0:1024], wait=[(cT, bc_ct)], inc=cV)
                sub_cv = cV.v
                E("vector", "tensor_mul", sqt[:], sqt[:], ps[3][:, 0:1024], inc=cV)
                mult_cv = cV.v
                bc_free = [sub_cv, mult_cv]
                E("vector", "tensor_scalar", self.GT_v[:, hc], sqt[:],
                  self.gm_sb[:, h:h + 1], self.bt_sb[:, h:h + 1], OP.mult, OP.add,
                  inc=cV)
                gt_hist.append(cV.v)
                sq_free = getattr(self, "_gn_sqfree", [None, None])
                sq_free[pr] = stats_ct
                self._gn_sqfree = sq_free
            last_gT_cs = cV.v
            self._gn_sqfree = [None, None]
            woev_hist = []
            prev_x2_cs = None
            prev_dg = 0
            for tt in range(8):
                pk = ps[tt % 2]
                xqt = self.tiles[2 + tt % 2]
                w = []
                if tt >= 2:
                    w.append((cV, woev_hist[tt - 2][0]))
                E("scalar", "dma_start", out=xqt[:],
                  in_=self.xq[(g * 1024 + tt * 128):(g * 1024 + (tt + 1) * 128), :],
                  wait=w if w else None, inc=dS, incv=16)
                xq_dma = dS.v
                w = [(cV, last_gT_cs)]
                if self.lite & 128:
                    w += [(vev, vev.v), (spp, spp.v), (vsd, vsd.v), (tpl, tpl.v),
                          (tS, tS.v), (cS, cS.v), (dS, dS.v), (dG, self.dG.v)]
                if tt >= 2:
                    w.append((cV, woev_hist[tt - 2][1]))
                first = True
                for hh in range(8):
                    for oc_i in range(2):
                        E("tensor", "matmul", pk[:, oc_i * 512:(oc_i + 1) * 512],
                          self.GT_v[:, hh * 1024 + tt * 128:hh * 1024 + tt * 128 + 128],
                          self.Wo_sb[:, hh * 1024 + oc_i * 512:hh * 1024 + oc_i * 512 + 512],
                          start=(hh == 0), stop=(hh == 7),
                          wait=w if first else None,
                          inc=cT if (hh == 7 and oc_i == 1) else None)
                        first = False
                wo_ct = cT.v
                wv = [(cT, wo_ct), (dS, xq_dma)]
                if prev_x2_cs is not None:
                    wv.append((cS, prev_x2_cs))
                    wv.append((dG, prev_dg))
                E("vector", "tensor_add", self.tiles[1][:], pk[:, 0:1024],
                  self.bo_rep[:], wait=wv, inc=cV)
                E("vector", "tensor_add", self.tiles[0][:], self.tiles[1][:],
                  xqt[:], inc=cV)
                ev0 = cV.v
                dbg_wait = []
                if "dbg_a" in self.dbg_outs and g == 0:
                    E("gpsimd", "dma_start",
                      out=self.dbg_outs["dbg_a"][:, tt * 1024:(tt + 1) * 1024],
                      in_=self.tiles[0][:], wait=[(cV, ev0)], inc=dG, incv=16)
                    dbg_wait = [(dG, dG.v)]
                E("vector", "reduce_sum", self.colA[:], self.tiles[0][:], axis=AX.X, inc=cV)
                E("vector", "tensor_scalar_mul", self.colA[:], self.colA[:],
                  -1.0 / 1024, wait=[(cV, cV.v)], inc=cV)
                neg_cv = cV.v
                E("scalar", "add", self.tiles[1][:], self.tiles[0][:],
                  self.colA[:, 0:1], wait=[(cV, neg_cv)] + dbg_wait, inc=cS)
                c_cs = cS.v
                E("vector", "tensor_mul", self.tiles[0][:], self.tiles[1][:],
                  self.tiles[1][:], wait=[(cS, c_cs)], inc=cV)
                E("vector", "reduce_sum", self.colB[:], self.tiles[0][:], axis=AX.X, inc=cV)
                E("vector", "tensor_scalar", self.colB[:], self.colB[:],
                  1.0 / 1024, EPS, OP.mult, OP.add, wait=[(cV, cV.v)], inc=cV)
                var_cv = cV.v
                dwait2 = []
                if "dbg_l" in self.dbg_outs and g == 0 and tt == 0:
                    E("gpsimd", "dma_start", out=AP(self.dbg_outs["dbg_l"].tensor, 0, [[1, 128], [1, 1]]),
                      in_=self.colA[:], wait=[(cV, neg_cv), (cS, cS.v)], inc=dG, incv=16)
                    E("gpsimd", "dma_start", out=AP(self.dbg_outs["dbg_l"].tensor, 128, [[1, 128], [1, 1]]),
                      in_=self.colB[:], wait=[(cV, var_cv)], inc=dG, incv=16)
                    dwait2 = [(dG, dG.v)]
                E("scalar", "activation", self.colA[:], self.colB[:], AF.Sqrt,
                  wait=[(cV, var_cv)] + dwait2, inc=cS)
                E("vector", "reciprocal", self.colB[:], self.colA[:],
                  wait=[(cS, cS.v)], inc=cV)
                rstd_cv = cV.v
                if "dbg_l" in self.dbg_outs and g == 0 and tt == 0:
                    E("gpsimd", "dma_start", out=AP(self.dbg_outs["dbg_l"].tensor, 256, [[1, 128], [1, 1]]),
                      in_=self.colB[:], wait=[(cV, rstd_cv)], inc=dG, incv=16)
                    dwait3 = [(dG, dG.v)]
                else:
                    dwait3 = []
                E("scalar", "mul", self.tiles[0][:], self.tiles[1][:],
                  self.colB[:, 0:1], wait=[(cV, rstd_cv)] + dwait3, inc=cS)
                x2_cs = cS.v
                E("vector", "tensor_copy", self.x2bt[:], self.tiles[0][:],
                  wait=[(cS, x2_cs), (dG, dG.v)], inc=cV)
                ev1 = cV.v
                row0 = g * 1024 + tt * 128
                E("gpsimd", "dma_start", out=self.x2f.ap()[row0:row0 + 128, :],
                  in_=self.tiles[0][:], wait=[(cV, ev1)], inc=dG, incv=16)
                E("gpsimd", "dma_start", out=self.x2b.ap()[row0:row0 + 128, :],
                  in_=self.x2bt[:], wait=[(cV, ev1)], inc=dG, incv=16)
                woev_hist.append((ev0, ev1))
                prev_x2_cs = x2_cs
                prev_dg = dG.v
            grp_vec_done = cV.v
        self.g_x2_written = dG.v
        if "dbg_x2" in self.dbg_outs:
            E("gpsimd", "dma_start", out=self.dbg_outs["dbg_x2"], in_=self.x2f.ap(),
              wait=[(dG, self.g_x2_written)], inc=dG, incv=16)


    def _ph5(self):
        E = self.E
        dS, dG, dT, dO = self.dS, self.dG, self.dT, self.dO
        cS, cV, cT = self.cS, self.cV, self.cT
        ps = self.ps
        wb = lambda s: self.Wo_sb[:, s * 4096:(s + 1) * 4096]

        ffnA = self.tiles[2]
        ffnB = self.tiles[3]
        ph4_ct = cT.v
        wdma_hist = [ph4_ct, ph4_ct]
        ab_mult_hist = cV.v
        ln2_hist = [(cS.v, 0), (cS.v, 0)]
        last_w3_cv = cV.v

        prev_w3_region = [cV.v] * 4
        for ch in range(4):
            w = [(dG, self.g_x2_written)]
            for ib in range(8):
                E("sync", "dma_start", out=self.X2T_v[:, ib * 1024:(ib + 1) * 1024],
                  in_=self.x2b.ap()[ch * 1024:(ch + 1) * 1024, ib * 128:(ib + 1) * 128],
                  transpose=True, wait=w if ib == 0 else None, inc=dT, incv=16)
            x2t_dT = dT.v

            for ob in range(32):
                slot = ob % 2
                E("sync", "dma_start", out=wb(slot)[:, 0:1024],
                  in_=self.wts["w1"].ap()[ob * 131072:(ob + 1) * 131072],
                  wait=[(cT, wdma_hist[slot])], inc=dT, incv=16)
                E("sync", "dma_start", out=wb(slot)[:, 1024:2048],
                  in_=self.wts["w2"].ap()[ob * 131072:(ob + 1) * 131072],
                  inc=dT, incv=16)
                wdT = dT.v
                fills = []
                for wi in range(2):
                    pk, pw = self._psum_fill_wait()
                    pw = pw + [(dT, wdT), (dT, x2t_dT)]
                    if ch == 0 and ob == 0 and wi == 0:
                        pw.append((cV, ab_mult_hist))
                    if ch > 0 and ob < 2:
                        pw.append((cV, prev_w3_region[self.ps.index(pk)]))
                    for tp in range(2):
                        for ib in range(8):
                            E("tensor", "matmul", pk[:, tp * 512:(tp + 1) * 512],
                              wb(slot)[:, wi * 1024 + ib * 128:wi * 1024 + ib * 128 + 128],
                              self.X2T_v[:, ib * 1024 + tp * 512:ib * 1024 + tp * 512 + 512],
                              start=(ib == 0), stop=(ib == 7),
                              wait=pw if (tp == 0 and ib == 0) else None,
                              inc=cT if (tp == 1 and ib == 7) else None)
                    fills.append((pk, cT.v))
                wdma_hist[slot] = cT.v
                w = [(cT, fills[0][1]), (cV, ab_mult_hist)]
                E("scalar", "activation", ffnA[:], fills[0][0][:, 0:1024], AF.Silu,
                  scale=self.s_sb[:, ob:ob + 1], wait=w, inc=cS)
                self.evac_hist.append(cS.v)
                E("scalar", "mul", ffnB[:], fills[1][0][:, 0:1024],
                  self.s_sb[:, 32 + ob:32 + ob + 1], wait=[(cT, fills[1][1])], inc=cS)
                self.evac_hist.append(cS.v)
                E("vector", "tensor_mul", self.A_v[:, ob * 1024:(ob + 1) * 1024],
                  ffnA[:], ffnB[:], wait=[(cS, cS.v)], inc=cV)
                ab_mult_hist = cV.v
            a_done_cv = cV.v

            h_done_cs = 0
            for hname, src_v, dst_v, scol in [("h1", self.A_v, self.B_v, 64),
                                              ("h2", self.B_v, self.A_v, 96)]:
                for ob in range(32):
                    slot = ob % 2
                    E("sync", "dma_start", out=wb(slot)[:],
                      in_=self.wts[hname].ap()[ob * 524288:(ob + 1) * 524288],
                      wait=[(cT, wdma_hist[slot])], inc=dT, incv=16)
                    wdT = dT.v
                    pk, pw = self._psum_fill_wait()
                    pw = pw + [(dT, wdT)]
                    if ob == 0:
                        if hname == "h1":
                            pw.append((cV, a_done_cv))
                        else:
                            pw.append((cS, h_done_cs))
                    for tp in range(2):
                        for ib in range(32):
                            E("tensor", "matmul", pk[:, tp * 512:(tp + 1) * 512],
                              wb(slot)[:, ib * 128:ib * 128 + 128],
                              src_v[:, ib * 1024 + tp * 512:ib * 1024 + tp * 512 + 512],
                              start=(ib == 0), stop=(ib == 31),
                              wait=pw if (tp == 0 and ib == 0) else None,
                              inc=cT if (tp == 1 and ib == 31) else None)
                    wdma_hist[slot] = cT.v
                    E("scalar", "activation", dst_v[:, ob * 1024:(ob + 1) * 1024],
                      pk[:, 0:1024], AF.Gelu, scale=self.s_sb[:, scol + ob:scol + ob + 1],
                      wait=[(cT, cT.v)], inc=cS)
                    self.evac_hist.append(cS.v)
                h_done_cs = cS.v

            h2_mm_ct = cT.v
            for j in range(8):
                loc0 = ch * 1024 + j * 128
                E("scalar", "dma_start", out=self.Bf_v[:, j * 1024:(j + 1) * 1024],
                  in_=self.x2f.ap()[loc0:loc0 + 128, :],
                  wait=[(dG, self.g_x2_written), (cT, h2_mm_ct)] if j == 0 else None,
                  inc=dS, incv=16)
            x2r_dS = dS.v
            w3_region_cv = [cV.v] * 4
            for half in range(2):
                for ib in range(32):
                    slot = ib % 2
                    E("sync", "dma_start", out=wb(slot)[:, 0:1024],
                      in_=self.wts["w3"].ap()[ib * 131072:(ib + 1) * 131072],
                      wait=[(cS, h_done_cs), (cT, wdma_hist[slot])], inc=dT, incv=16)
                    wdT = dT.v
                    for tt in range(4):
                        pk = ps[tt]
                        pw = None
                        if ib == 0:
                            pw = [(dT, wdT), (cS, self.evac_hist[-1])]
                            if half == 1:
                                pw.append((cV, w3_region_cv[tt]))
                        elif tt == 0:
                            pw = [(dT, wdT)]
                        for oc_i in range(2):
                            E("tensor", "matmul", pk[:, oc_i * 512:(oc_i + 1) * 512],
                              self.A_v[:, ib * 1024 + (half * 4 + tt) * 128:
                                       ib * 1024 + (half * 4 + tt) * 128 + 128],
                              wb(slot)[:, oc_i * 512:(oc_i + 1) * 512],
                              start=(ib == 0), stop=(ib == 31),
                              wait=pw if oc_i == 0 else None,
                              inc=cT if (ib == 31 and tt == 3 and oc_i == 1) else None)
                w3_ct = cT.v
                wdma_hist[0] = w3_ct
                wdma_hist[1] = w3_ct
                for tt in range(4):
                    pk = ps[tt]
                    pr = tt % 2
                    R2 = self.tiles[0 + pr]
                    C2 = self.tiles[2 + pr]
                    cX = self.colA if pr == 0 else self.colC
                    cY = self.colB if pr == 0 else self.colD
                    j = half * 4 + tt
                    loc0 = ch * 1024 + j * 128
                    wv = [(cT, w3_ct), (dS, x2r_dS)]
                    if ln2_hist[pr][0] is not None:
                        wv.append((cS, ln2_hist[pr][0]))
                        wv.append((dO, ln2_hist[pr][1]))
                    E("vector", "tensor_mul", R2[:], pk[:, 0:1024],
                      self.w3s_rep[:], wait=wv, inc=cV)
                    E("vector", "tensor_add", R2[:], R2[:],
                      self.Bf_v[:, j * 1024:(j + 1) * 1024], inc=cV)
                    w3_region_cv[tt] = cV.v
                    E("vector", "reduce_sum", cX[:], R2[:], axis=AX.X, inc=cV)
                    E("vector", "tensor_scalar_mul", cX[:], cX[:],
                      -1.0 / 1024, wait=[(cV, cV.v)], inc=cV)
                    neg_cv = cV.v
                    E("scalar", "add", C2[:], R2[:], cX[:, 0:1],
                      wait=[(cV, neg_cv)], inc=cS)
                    c_cs = cS.v
                    E("vector", "tensor_mul", R2[:], C2[:], C2[:],
                      wait=[(cS, c_cs)], inc=cV)
                    E("vector", "reduce_sum", cY[:], R2[:], axis=AX.X, inc=cV)
                    E("vector", "tensor_scalar", cY[:], cY[:],
                      1.0 / 1024, EPS, OP.mult, OP.add, wait=[(cV, cV.v)], inc=cV)
                    var_cv = cV.v
                    E("scalar", "activation", cX[:], cY[:], AF.Sqrt,
                      wait=[(cV, var_cv)], inc=cS)
                    E("vector", "reciprocal", cY[:], cX[:],
                      wait=[(cS, cS.v)], inc=cV)
                    rstd_cv = cV.v
                    E("scalar", "mul", R2[:], C2[:], cY[:, 0:1],
                      wait=[(cV, rstd_cv)], inc=cS)
                    x2_cs = cS.v
                    E("gpsimd", "dma_start", out=self.y[loc0:loc0 + 128, :],
                      in_=R2[:], wait=[(cS, x2_cs)], inc=dO, incv=16)
                    ln2_hist[pr] = (x2_cs, dO.v)
            prev_w3_region = list(w3_region_cv)

    def _finish(self):
        if self.lite & 64:
            return
        self.W("gpsimd", [(self.dG, self.dG.v)])
        self.W("sync", [(self.dT, self.dT.v)])
        self.W("scalar", [(self.dS, self.dS.v), (self.dC, self.c_consts), (self.dO, self.dO.v)])

    def run(self, in_maps, core_ids, **kw):
        with self.st:
            with self.nc.Block() as block:
                @block.tensor
                def _(e):
                    for fn in self.plan.q["tensor"]:
                        fn(e)

                @block.vector
                def _(e):
                    for fn in self.plan.q["vector"]:
                        fn(e)

                @block.scalar
                def _(e):
                    for fn in self.plan.q["scalar"]:
                        fn(e)

                @block.sync
                def _(e):
                    for fn in self.plan.q["sync"]:
                        fn(e)

                @block.gpsimd
                def _(e):
                    for fn in self.plan.q["gpsimd"]:
                        fn(e)
            return run_bass_kernel_spmd(self.nc, in_maps, core_ids=list(core_ids), **kw)



def make_core_inputs(inputs, b, p):
    X = np.asarray(inputs["L0_attn_x"])
    shift = -128 * (1 - p)
    n = np.arange(NQ, dtype=np.float64) - NEG + shift
    nramp = np.tile(np.maximum(n, 0.0).astype(np.float32), (8, 1))
    ind = (n >= 0).astype(np.float32)[None, :]
    pp, qq = np.mgrid[0:128, 0:640]
    m2 = np.where(qq - pp - 256 + shift < 0, np.float32(NEGINF), np.float32(0.0))
    xfull = np.ascontiguousarray(X[b])
    xq = np.ascontiguousarray(X[b].reshape(64, 128, D)[p::2].reshape(TOK, D))
    return {
        "x": xfull, "xq": xq,
        "wqm": np.ascontiguousarray(inputs["L0_attn_Wq"]),
        "wkm": np.ascontiguousarray(inputs["L0_attn_Wk"]),
        "wvm": np.ascontiguousarray(inputs["L0_attn_Wv"]),
        "wom": np.ascontiguousarray(inputs["L0_attn_Wo"]),
        "bo": np.asarray(inputs["L0_attn_bo"], np.float32).reshape(1, D),
        "theta": np.asarray(inputs["L0_attn_theta"], np.float32).reshape(8, 1),
        "gamma": np.ascontiguousarray(np.asarray(inputs["L0_attn_gamma"], np.float32).reshape(8, 128).T),
        "beta": np.ascontiguousarray(np.asarray(inputs["L0_attn_beta"], np.float32).reshape(8, 128).T),
        "w1m": np.ascontiguousarray(inputs["L0_ffn_w1_w"]),
        "s1": np.ascontiguousarray(np.asarray(inputs["L0_ffn_w1_s"], np.float32).reshape(32, 128).T),
        "w2m": np.ascontiguousarray(inputs["L0_ffn_w2_w"]),
        "s2": np.ascontiguousarray(np.asarray(inputs["L0_ffn_w2_s"], np.float32).reshape(32, 128).T),
        "h1m": np.ascontiguousarray(inputs["L0_ffn_h1_w"]),
        "sh1": np.ascontiguousarray(np.asarray(inputs["L0_ffn_h1_s"], np.float32).reshape(32, 128).T),
        "h2m": np.ascontiguousarray(inputs["L0_ffn_h2_w"]),
        "sh2": np.ascontiguousarray(np.asarray(inputs["L0_ffn_h2_s"], np.float32).reshape(32, 128).T),
        "w3m": np.ascontiguousarray(inputs["L0_ffn_w3_w"]),
        "w3s": np.asarray(inputs["L0_ffn_w3_s"], np.float32).reshape(1, D),
        "nramp": nramp, "ind": ind, "m2": m2,
        "o8": np.full((8, 1), 0.125, np.float32),
        "oc": np.ones((1, 128), np.float32),
        "o128f": np.full((128, 1), 1.0 / 128, np.float32),
        "o128b": np.ones((128, 1), ml_dtypes.bfloat16),
    }


def kernel(**inputs):
    b = Builder()
    in_maps = [make_core_inputs(inputs, c // 2, c % 2) for c in range(8)]
    res = b.run(in_maps, core_ids=list(range(8)))
    X = np.asarray(inputs["L0_attn_x"])
    out = np.empty((4, T, D), np.float32)
    for c in range(8):
        bb, p = c // 2, c % 2
        out[bb].reshape(64, 128, D)[p::2] = res.results[c]["y"].reshape(32, 128, D)
    return out


# version 22
# speedup vs baseline: 1.0840x; 1.0250x over previous
import math
import contextlib
import numpy as np
import ml_dtypes

import concourse.bass as bass
import concourse.mybir as mybir
from concourse.ap import AP
from concourse.bass_utils import run_bass_kernel_spmd

F32 = mybir.dt.float32
BF16 = mybir.dt.bfloat16
AF = mybir.ActivationFunctionType
OP = mybir.AluOpType
AX = mybir.AxisListType

D = 1024
T = 8192
H = 8
HD = 128
TOK = 4096
NG = 4
GT = 1024
NCH = 64
DH = 4096
TC = 1024
NEG = 256
NQ = 8448
T2C = 8576
ISQ = 1.0 / math.sqrt(HD)
EPS = 1e-5
NEGINF = -1e30

ENGINES = ("tensor", "vector", "scalar", "sync", "gpsimd")


class Plan:
    def __init__(self):
        self.q = {e: [] for e in ENGINES}

    def add(self, e, fn):
        self.q[e].append(fn)


class Sem:
    def __init__(self, nc, st, name):
        self.h = st.enter_context(nc.semaphore(name))
        self.v = 0

    def take(self, k=1):
        self.v += k
        return self.v


class Builder:
    def __init__(self, dbg=(), phases=99, lite=0):
        self.lite = lite
        self.dbg = set(dbg)
        self.phases = phases
        self.nc = bass.Bass("TRN2", target_bir_lowering=False)
        self.st = contextlib.ExitStack()
        self.plan = Plan()
        self._build()

    def E(self, e, meth, *a, wait=None, inc=None, incv=1, **k):
        if inc is not None:
            inc.take(incv)
        def fn(eng, meth=meth, a=a, k=k, wait=wait, inc=inc, incv=incv):
            if wait:
                for s, v in wait:
                    eng.wait_ge(s.h, v)
            inst = getattr(eng, meth)(*a, **k)
            if inc is not None:
                inst.then_inc(inc.h, incv)
        self.plan.add(e, fn)

    def W(self, e, waits):
        def fn(eng, waits=waits):
            for s, v in waits:
                eng.wait_ge(s.h, v)
        self.plan.add(e, fn)

    def barrier(self):
        tot = self.bar.take(4)
        ctv = self.cT.v
        for e in ENGINES:
            if e == "tensor":
                def fn(eng, tot=tot):
                    eng.wait_ge(self.bar.h, tot)
                self.plan.add(e, fn)
            else:
                def fn(eng, tot=tot, ctv=ctv):
                    eng.nop().then_inc(self.bar.h, 1)
                    eng.wait_ge(self.bar.h, tot)
                    eng.wait_ge(self.cT.h, ctv)
                self.plan.add(e, fn)

    def _build(self):
        nc, st = self.nc, self.st

        def din(name, shape, dt=F32):
            return nc.dram_tensor(name, shape, dt, kind="ExternalInput").ap()

        def scr(name, shape, dt):
            return nc.dram_tensor(name, shape, dt, kind="Internal")

        self.x = din("x", [T, D])
        self.xq = din("xq", [TOK, D])
        self.Wq = din("wqm", [D, D]); self.Wk = din("wkm", [D, D])
        self.Wv = din("wvm", [D, D]); self.Wo = din("wom", [D, D])
        self.bo = din("bo", [1, D])
        self.theta = din("theta", [8, 1])
        self.gamma = din("gamma", [128, 8]); self.beta = din("beta", [128, 8])
        self.w1 = din("w1m", [DH, D]); self.w2 = din("w2m", [DH, D])
        self.h1 = din("h1m", [DH, DH]); self.h2 = din("h2m", [DH, DH])
        self.w3 = din("w3m", [D, DH])
        self.s1 = din("s1", [128, 32]); self.s2 = din("s2", [128, 32])
        self.sh1 = din("sh1", [128, 32]); self.sh2 = din("sh2", [128, 32])
        self.w3s = din("w3s", [1, D])
        self.nramp = din("nramp", [8, NQ])
        self.ind = din("ind", [1, NQ])
        self.m2 = din("m2", [128, 640])
        self.o8 = din("o8", [8, 1])
        self.oc = din("oc", [1, 128])
        self.o128f = din("o128f", [128, 1])
        self.o128b = din("o128b", [128, 1], BF16)
        self.y = nc.dram_tensor("y", [TOK, D], F32, kind="ExternalOutput").ap()

        self.xb = scr("xb", [T, D], BF16)
        self.xqb = scr("xqb", [TOK, D], BF16)
        self.dflat = scr("dflat", [NQ], F32)
        self.t2d = scr("t2d", [128 * T2C], F32)
        self.ktS = scr("ktS", [NCH * 128 * 1024], BF16)
        self.vS = scr("vS", [NCH * 128 * 1024], BF16)
        self.qS = scr("qS", [NG * 128 * 8192], BF16)
        self.x2f = scr("x2f", [TOK, D], F32)
        self.x2b = scr("x2b", [TOK, D], BF16)
        self.wbs = {
            "w1": scr("wb1", [DH, D], BF16), "w2": scr("wb2", [DH, D], BF16),
            "h1": scr("wbh1", [DH, DH], BF16), "h2": scr("wbh2", [DH, DH], BF16),
            "w3": scr("wb3", [D, DH], BF16),
        }
        self.wts = {
            "w1": scr("w1t", [32 * 128 * 8 * 128], BF16),
            "w2": scr("w2t", [32 * 128 * 8 * 128], BF16),
            "h1": scr("h1t", [32 * 128 * 32 * 128], BF16),
            "h2": scr("h2t", [32 * 128 * 32 * 128], BF16),
            "w3": scr("w3t", [32 * 128 * 1024], BF16),
        }

        self.dbg_outs = {}
        shapes = {
            "dbg_kt": ([NCH * 128 * 1024], BF16),
            "dbg_v": ([NCH * 128 * 1024], BF16),
            "dbg_q": ([NG * 128 * 8192], BF16),
            "dbg_t2": ([128 * T2C], F32),
            "dbg_o": ([128, 8192], F32),
            "dbg_l": ([1, 8 * GT], F32),
            "dbg_g": ([128, 8192], BF16),
            "dbg_x2": ([TOK, D], F32),
            "dbg_w1t": ([32 * 128 * 8 * 128], BF16),
            "dbg_a": ([128, 8192], F32),
        }
        for name in self.dbg:
            shp, dt = shapes[name]
            self.dbg_outs[name] = nc.dram_tensor(name, shp, dt, kind="ExternalOutput").ap()

        sbuf = lambda name, shape, dt: st.enter_context(nc.sbuf_tensor(name, shape, dt))

        AR_COLS = 73984
        self.arena = sbuf("arena", [128, AR_COLS], BF16)
        arena_f = self.arena.bitcast(F32)

        def cb(c0, n):
            return self.arena[:, c0:c0 + n]

        def cf(c0, n):
            return arena_f[:, c0 // 2:c0 // 2 + n]

        self.T2f = cf(0, T2C)
        self.ind_sb = cf(0, NQ)[0:1, :]
        self.NRf = cf(20736, NQ)[0:8, :]
        self.Dtab_sb = cf(20736, NQ)[0:1, :]
        self.EXf = cf(41216, NQ)[0:8, :]
        self.kstage = [cb(17152, 512), cb(17664, 512)]
        self.vstage = [cb(18176, 1024), cb(19200, 1024)]
        self.lcarv = [cf(17152, 1024), cf(19200, 1024), cf(21248, 1024)]
        self.XTc = [cb(24832, 8192), cb(33024, 8192)]
        self.Wqkv = cb(41216, 16384)
        self.wtstage = [cb(65792, 4096), cb(69888, 4096)]
        self.O_sb = cf(24832, 8192)
        self.Qg_v = cb(41216, 8192)
        self.Kbuf = [cb(49408, 4096), cb(53504, 4096)]
        self.Vbuf = [cb(57600, 4096), cb(61696, 4096)]
        self.GT_v = cb(65792, 8192)
        self.A_v = cb(0, 32768)
        self.B_v = cb(32768, 32768)
        self.Bf_v = cf(32768, 16384)
        self.X2T_v = cb(65536, 8192)

        self.Wo_sb = sbuf("wo_sb", [128, 8192], BF16)
        self.M2sb = sbuf("m2sb", [128, 640], F32)
        self.th_sb = sbuf("th_sb", [8, 1], F32)
        self.nth_sb = sbuf("nth_sb", [8, 1], F32)
        self.o8_sb = sbuf("o8_sb", [8, 1], F32)
        self.oc_sb = sbuf("oc_sb", [1, 128], F32)
        self.o128f_sb = sbuf("o128f_sb", [128, 1], F32)
        self.o128b_sb = sbuf("o128b_sb", [128, 1], BF16)
        self.gm_sb = sbuf("gm_sb", [128, 8], F32)
        self.bt_sb = sbuf("bt_sb", [128, 8], F32)
        self.s_sb = sbuf("s_sb", [128, 128], F32)
        self.bo_rep = sbuf("bo_rep", [128, D], F32)
        self.w3s_rep = sbuf("w3s_rep", [128, D], F32)
        self.bo_row = self.bo_rep[0:1, :]
        self.w3s_row = self.w3s_rep[0:1, :]
        self.SD = [sbuf("sd0", [128, 1024], F32), sbuf("sd1", [128, 1024], F32)]
        self.Pb = [sbuf("pb0", [128, 1024], BF16), sbuf("pb1", [128, 1024], BF16)]
        self.tiles = [sbuf(f"tile{i}", [128, 1024], F32) for i in range(4)]
        self.x2bt = sbuf("x2bt", [128, 1024], BF16)
        self.colA = sbuf("colA", [128, 1], F32)
        self.colB = sbuf("colB", [128, 1], F32)
        self.colC = sbuf("colC", [128, 1], F32)
        self.colD = sbuf("colD", [128, 1], F32)
        self.wbuf = [self.Wo_sb[:, 0:4096], self.Wo_sb[:, 4096:8192]]

        self.ps = [st.enter_context(nc.psum_tensor(f"ps{i}", [128, 1024], F32)) for i in range(4)]

        s = lambda name: Sem(nc, st, name)
        self.dC = s("dC"); self.dG = s("dG"); self.dT = s("dT"); self.dS = s("dS")
        self.dW = s("dW"); self.dO = s("dO")
        self.cT = s("cT"); self.cV = s("cV"); self.cS = s("cS")
        self.vsd = s("vsd"); self.spp = s("spp"); self.tpl = s("tpl")
        self.tS = s("tS"); self.gdn = s("gdn")
        self.vev = s("vev"); self.tblk = s("tblk"); self.bar = s("bar")
        self.tgrp = s("tgrp")

        self.fill_k = 0
        self.evac_hist = []

        self._ph0()
        if not (self.lite & 32):
            self.barrier()
        if self.phases >= 1:
            self._ph1()
        if self.phases >= 2:
            self._ph25()
            self._ph2()
            self.barrier()
        if self.phases >= 3:
            self._ph3()
            self.barrier()
        if self.phases >= 4:
            self._ph4()
        if self.phases >= 5:
            self._ph5()
        self._finish()

    def _ph0(self):
        E, W = self.E, self.W
        dC, dG, dS, cT, cV, cS = self.dC, self.dG, self.dS, self.cT, self.cV, self.cS
        E("scalar", "dma_start", out=self.th_sb[:], in_=self.theta, inc=dC, incv=16)
        th_done = dC.v
        const_list = [] if (self.lite & 1) else None
        for dst, src in ([(self.o8_sb[:], self.o8)] if (self.lite & 1) else
                        [(self.M2sb[:], self.m2), (self.o8_sb[:], self.o8),
                         (self.oc_sb[:], self.oc), (self.o128f_sb[:], self.o128f),
                         (self.o128b_sb[:], self.o128b), (self.bo_row[:], self.bo),
                         (self.w3s_row[:], self.w3s)]) + [(self.ind_sb[:], self.ind)]:
            E("scalar", "dma_start", out=dst, in_=src, inc=dC, incv=16)
        if not (self.lite & 1):
            E("scalar", "dma_start", out=self.gm_sb[:], in_=self.gamma, inc=dC, incv=16)
            E("scalar", "dma_start", out=self.bt_sb[:], in_=self.beta, inc=dC, incv=16)
            for i, t in enumerate([self.s1, self.s2, self.sh1, self.sh2]):
                E("scalar", "dma_start", out=self.s_sb[:, i * 32:(i + 1) * 32],
                  in_=t, inc=dC, incv=16)
        self.c_consts = dC.v
        E("gpsimd", "dma_start", out=self.NRf, in_=self.nramp, inc=dG, incv=16)
        g_nr = dG.v

        if not (self.lite & 4):
            E("vector", "tensor_scalar_mul", self.nth_sb[:], self.th_sb[:], -1.0,
              wait=[(dC, th_done)], inc=cV)
            v_nth = cV.v
            E("scalar", "activation", self.EXf, self.NRf, AF.Exp, scale=self.nth_sb[0:8, 0:1],
              wait=[(dG, g_nr), (cV, v_nth)], inc=cS)
        s_expn = cS.v
        mul_hist = []
        pieces = [] if (self.lite & 8) else [(k * 512, 512) for k in range(NQ // 512)]
        if NQ % 512 and not (self.lite & 8):
            pieces.append((NQ - NQ % 512, NQ % 512))
        for k, (c0, wdt) in enumerate(pieces):
            pk = self.ps[k % 2]
            w = [(cS, s_expn), (dC, self.c_consts)] if k < 2 else [(cV, mul_hist[k - 2])]
            E("tensor", "matmul", pk[0:1, 0:wdt], self.o8_sb[:],
              self.EXf[:, c0:c0 + wdt], start=True, stop=True, wait=w, inc=cT)
            E("vector", "tensor_mul", self.Dtab_sb[:, c0:c0 + wdt],
              pk[0:1, 0:wdt], self.ind_sb[:, c0:c0 + wdt],
              wait=[(cT, cT.v)], inc=cV)
            mul_hist.append(cV.v)
        if self.lite & 16:
            self.s_t2sb = dS.v
            return
        E("gpsimd", "dma_start", out=self.dflat.ap(), in_=self.Dtab_sb,
          wait=[(cV, cV.v)], inc=dG, incv=16)
        E("gpsimd", "dma_start",
          out=AP(self.t2d.ap().tensor, 0, [[T2C + 1, 128], [1, NQ]]),
          in_=AP(self.dflat.ap().tensor, 0, [[0, 128], [1, NQ]]),
          wait=[(dG, dG.v)], inc=dG, incv=16)
        g_t2d = dG.v
        E("scalar", "dma_start", out=self.T2f,
          in_=AP(self.t2d.ap().tensor, 0, [[T2C, 128], [1, T2C]]),
          wait=[(dG, g_t2d)], inc=dS, incv=16)
        self.s_t2sb = dS.v
        if "dbg_t2" in self.dbg_outs:
            E("gpsimd", "dma_start", out=self.dbg_outs["dbg_t2"], in_=self.t2d.ap(),
              wait=[(dG, g_t2d)], inc=dG, incv=16)

        for rep, row, pp in ([] if (self.lite & 2) else [(self.bo_rep, self.bo_row, 2), (self.w3s_rep, self.w3s_row, 3)]):
            for half in range(2):
                E("tensor", "matmul", self.ps[pp][:, half * 512:(half + 1) * 512],
                  self.oc_sb[:], row[0:1, half * 512:(half + 1) * 512],
                  start=True, stop=True, wait=[(dC, self.c_consts)], inc=cT)
                E("vector", "tensor_copy", rep[:, half * 512:(half + 1) * 512],
                  self.ps[pp][:, half * 512:(half + 1) * 512], wait=[(cT, cT.v)], inc=cV)
        if "dbg_o" in self.dbg_outs and self.phases < 1:
            E("gpsimd", "dma_start", out=self.dbg_outs["dbg_o"][:, 0:1024],
              in_=self.bo_rep[:], wait=[(cV, cV.v)], inc=dG, incv=16)
            E("gpsimd", "dma_start", out=self.dbg_outs["dbg_o"][:, 1024:2048],
              in_=self.w3s_rep[:], wait=[(cV, cV.v)], inc=dG, incv=16)

    def _ph1(self):
        E = self.E
        dG = self.dG
        self.g_xb_piece = []
        for i in range(8):
            E("gpsimd", "dma_start", out=self.xb.ap()[i * 1024:(i + 1) * 1024, :],
              in_=self.x[i * 1024:(i + 1) * 1024, :], inc=dG, incv=16)
            self.g_xb_piece.append(dG.v)
        self.g_xb = dG.v
        for i in range(4):
            E("gpsimd", "dma_start", out=self.xqb.ap()[i * 1024:(i + 1) * 1024, :],
              in_=self.xq[i * 1024:(i + 1) * 1024, :], inc=dG, incv=16)
        self.g_xqb = dG.v
        for wsrc, c0 in [(self.Wk, 0), (self.Wv, 8192)]:
            E("gpsimd", "dma_start", out=self.Wqkv[:, c0:c0 + 8192],
              in_=AP(wsrc.tensor, 0, [[1024, 128], [131072, 8], [1, 1024]]), inc=dG, incv=16)
        self.g_wkv = dG.v
        E("gpsimd", "dma_start", out=self.Wo_sb[:],
          in_=AP(self.Wo.tensor, 0, [[1024, 128], [131072, 8], [1, 1024]]), inc=dG, incv=16)
        self.g_wo = dG.v
        if "dbg_o" in self.dbg_outs and self.phases < 2:
            E("gpsimd", "dma_start", out=self.dbg_outs["dbg_o"],
              in_=self.Wo_sb[:], wait=[(dG, self.g_wo)], inc=dG, incv=16)

    def _psum_fill_wait(self):
        k = self.fill_k
        self.fill_k += 1
        if k >= 4:
            return self.ps[k % 4], [(self.cS, self.evac_hist[k - 4])]
        return self.ps[k % 4], []

    def _proj_pass(self, nchunks, src_dram, wcol0, scale, dst_writer, tdone_list):
        E = self.E
        dT, dG, cT, cS = self.dT, self.dG, self.cT, self.cS
        stage_slot_dma = [0, 0]
        for tc in range(nchunks):
            self._pump_prep(25)
            buf = self.XTc[tc % 2]
            w = [(dG, self.g_stream_ready)]
            if tc >= 2:
                w.append((cT, tdone_list[tc - 2]))
            for ib in range(8):
                E("sync", "dma_start", out=buf[:, ib * 1024:(ib + 1) * 1024],
                  in_=src_dram.ap()[tc * 1024:(tc + 1) * 1024, ib * 128:(ib + 1) * 128],
                  transpose=True, wait=w if ib == 0 else None, inc=dT, incv=16)
            tr_done = dT.v
            for o in range(8):
                for tp in range(2):
                    pk, pw = self._psum_fill_wait()
                    pw = pw + [(dT, tr_done)]
                    for i in range(8):
                        E("tensor", "matmul", pk[:, 0:512],
                          self.Wqkv[:, wcol0 + i * 1024 + o * 128:wcol0 + i * 1024 + o * 128 + 128],
                          buf[:, i * 1024 + tp * 512:i * 1024 + tp * 512 + 512],
                          start=(i == 0), stop=(i == 7),
                          wait=pw if i == 0 else None, inc=cT if i == 7 else None)
                    slot = (o * 2 + tp) % 2
                    ew = [(cT, cT.v)]
                    if stage_slot_dma[slot]:
                        ew.append((self.dS, stage_slot_dma[slot]))
                    if scale is None:
                        E("scalar", "copy", self.kstage[slot][:], pk[:, 0:512], wait=ew, inc=cS)
                    else:
                        E("scalar", "activation", self.kstage[slot][:], pk[:, 0:512],
                          AF.Copy, scale=scale, wait=ew, inc=cS)
                    self.evac_hist.append(cS.v)
                    dst_writer(tc, o, tp, self.kstage[slot])
                    stage_slot_dma[slot] = self.dS.v
            tdone_list.append(cT.v)

    def _ph2(self):
        E = self.E
        dG, cT, cS = self.dG, self.cT, self.cS
        self.g_stream_ready = self.g_xb
        tdone = []

        def write_k(tc, o, tp, stage):
            cbase = tc * 8 + tp * 4
            E("gpsimd", "dma_start",
              out=AP(self.ktS.ap().tensor, cbase * 131072 + o * 128,
                     [[1024, 128], [131072, 4], [1, 128]]),
              in_=stage[:], wait=[(cS, cS.v)], inc=dG, incv=16)

        dT = self.dT
        kslot_dma = [0, 0]
        vslot_dma = [0, 0]
        for tc in range(8):
            self._pump_prep(25)
            buf = self.XTc[tc % 2]
            w = [(dG, self.g_xb_piece[tc]), (dG, self.g_wkv)]
            if tc >= 2:
                w.append((cT, tdone[tc - 2]))
            for ib in range(8):
                E("sync", "dma_start", out=buf[:, ib * 1024:(ib + 1) * 1024],
                  in_=self.xb.ap()[tc * 1024:(tc + 1) * 1024, ib * 128:(ib + 1) * 128],
                  transpose=True, wait=w if ib == 0 else None, inc=dT, incv=16)
            tr_done = dT.v
            for o in range(8):
                for tp in range(2):
                    pk, pw = self._psum_fill_wait()
                    pw = pw + [(dT, tr_done)]
                    for i in range(8):
                        E("tensor", "matmul", pk[:, 0:512],
                          self.Wqkv[:, i * 1024 + o * 128:i * 1024 + o * 128 + 128],
                          buf[:, i * 1024 + tp * 512:i * 1024 + tp * 512 + 512],
                          start=(i == 0), stop=(i == 7),
                          wait=pw if i == 0 else None, inc=cT if i == 7 else None)
                    slot = tp
                    ew = [(cT, cT.v)]
                    if kslot_dma[slot]:
                        ew.append((self.dS, kslot_dma[slot]))
                    E("scalar", "copy", self.kstage[slot][:], pk[:, 0:512], wait=ew, inc=cS)
                    self.evac_hist.append(cS.v)
                    cbase = tc * 8 + tp * 4
                    E("scalar", "dma_start",
                      out=AP(self.ktS.ap().tensor, cbase * 131072 + o * 128,
                             [[1024, 128], [131072, 4], [1, 128]]),
                      in_=self.kstage[slot][:], inc=self.dS, incv=16)
                    kslot_dma[slot] = self.dS.v
            for tt in range(8):
                pk, pw = self._psum_fill_wait()
                for oc_i in range(2):
                    for i in range(8):
                        E("tensor", "matmul", pk[:, oc_i * 512:(oc_i + 1) * 512],
                          buf[:, i * 1024 + tt * 128:i * 1024 + tt * 128 + 128],
                          self.Wqkv[:, 8192 + i * 1024 + oc_i * 512:8192 + i * 1024 + oc_i * 512 + 512],
                          start=(i == 0), stop=(i == 7),
                          wait=pw if (i == 0 and oc_i == 0) else None,
                          inc=cT if (i == 7 and oc_i == 1) else None)
                slot = tt % 2
                ew = [(cT, cT.v)]
                if vslot_dma[slot]:
                    ew.append((self.dS, vslot_dma[slot]))
                E("scalar", "copy", self.vstage[slot][:], pk[:], wait=ew, inc=cS)
                self.evac_hist.append(cS.v)
                cix = tc * 8 + tt
                E("scalar", "dma_start", out=self.vS.ap()[cix * 131072:(cix + 1) * 131072],
                  in_=self.vstage[slot][:], inc=self.dS, incv=16)
                vslot_dma[slot] = self.dS.v
            tdone.append(cT.v)
        self.g_kv_written = None
        self.g_kv_written_dS = self.dS.v
        if "dbg_kt" in self.dbg_outs:
            E("gpsimd", "dma_start", out=self.dbg_outs["dbg_kt"], in_=self.ktS.ap(),
              wait=[(dG, self.g_kv_written)], inc=dG, incv=16)
        if "dbg_v" in self.dbg_outs:
            E("gpsimd", "dma_start", out=self.dbg_outs["dbg_v"], in_=self.vS.ap(),
              wait=[(dG, self.g_kv_written)], inc=dG, incv=16)

    def _ph3(self):
        E = self.E
        dG, cS = self.dG, self.cS
        E("gpsimd", "dma_start", out=self.Wqkv[:, 0:8192],
          in_=AP(self.Wq.tensor, 0, [[1024, 128], [131072, 8], [1, 1024]]), inc=dG, incv=16)
        self.g_stream_ready = dG.v
        tdone = []

        def write_q(tc, o, tp, stage):
            E("scalar", "dma_start",
              out=AP(self.qS.ap().tensor, tc * 1048576 + o * 1024 + tp * 512,
                     [[8192, 128], [1, 512]]),
              in_=stage[:], inc=self.dS, incv=16)

        self._proj_pass(4, self.xqb, 0, ISQ, write_q, tdone)
        self.g_q_written_dS = self.dS.v
        if "dbg_q" in self.dbg_outs:
            E("gpsimd", "dma_start", out=self.dbg_outs["dbg_q"], in_=self.qS.ap(),
              wait=[(dG, self.g_q_written)], inc=dG, incv=16)



    def _ph25(self):
        E = self.E
        dG = self.dG
        dT = self.dT
        items = []
        cast_done = {}

        self._stage_dma = [0, 0]
        self._prep_k = 0

        def mk_tr(nm, ib, nib, nob, obstride, rows):
            def f():
                slot = self._prep_k % 2
                st_v = self.wtstage[slot][:, 0:rows]
                w = [(dG, cast_done[nm])]
                if self._stage_dma[slot]:
                    w.append((dT, self._stage_dma[slot]))
                E("sync", "dma_start", out=st_v,
                  in_=self.wbs[nm].ap()[:, ib * 128:(ib + 1) * 128],
                  transpose=True, wait=w, inc=dT, incv=16)
                tr = dT.v
                if nm == "w3":
                    E("sync", "dma_start",
                      out=self.wts[nm].ap()[ib * 131072:(ib + 1) * 131072],
                      in_=st_v, wait=[(dT, tr)], inc=dT, incv=16)
                else:
                    E("sync", "dma_start",
                      out=AP(self.wts[nm].ap().tensor, ib * 128,
                             [[nib * 128, 128], [obstride, nob], [1, 128]]),
                      in_=st_v, wait=[(dT, tr)], inc=dT, incv=16)
                self._stage_dma[slot] = dT.v
                self._prep_k += 1
            return f

        cfg = {"w1": (self.w1, DH, D, 8, 32, 131072), "w2": (self.w2, DH, D, 8, 32, 131072),
               "h1": (self.h1, DH, DH, 32, 32, 524288), "h2": (self.h2, DH, DH, 32, 32, 524288),
               "w3": (self.w3, D, DH, 32, 0, 0)}
        for nm in ["w1", "w2", "h1", "h2", "w3"]:
            src_t, rows, cols, nib, nob, obstride = cfg[nm]
            npiece = max(1, (rows * cols * 4) // (8 << 20))
            rp = rows // npiece
            for k in range(npiece):
                E("gpsimd", "dma_start", out=self.wbs[nm].ap()[k * rp:(k + 1) * rp, :],
                  in_=src_t[k * rp:(k + 1) * rp, :], inc=dG, incv=16)
            cast_done[nm] = dG.v
            for ib in range(nib):
                items.append(mk_tr(nm, ib, nib, nob, obstride, rows))
        self._prep_items = items

    def _pump_prep(self, n):
        return

    def _prep_finalize(self):
        while self._prep_items:
            self._prep_items.pop(0)()
        self.prep_dG = self.dG.v
        self.prep_dT = self.dT.v
        self.prep_dS = self.dS.v

    def _flush_pv(self, pending, tpl_hist, spp_hist):
        E = self.E
        d = pending.pop(0)
        kslot = d["kk"] % 4
        pbb = self.Pb[kslot // 2][:, (kslot % 2) * 512:(kslot % 2) * 512 + 512]
        w = [(self.spp, spp_hist[d["kk"]])]
        if d["first"]:
            w.append((self.vev, self.vev.v))
            w.append((self.dS, d["vdma"]))
        E("tensor", "matmul", self.ps[2][:, d["p0"]:d["p1"]],
          self.Vbuf[d["gb"] % 2][:, d["ci"] * 1024 + d["h"] * 128:d["ci"] * 1024 + d["h"] * 128 + 128],
          pbb[:, 0:d["p1"] - d["p0"]],
          start=(d["ci"] == 0), stop=(d["ci"] == 3), wait=w)
        row = 0 if d["p0"] < 512 else 32
        E("tensor", "matmul", self.ps[3][row:row + 1, d["p0"]:d["p1"]],
          self.o128b_sb[:], pbb[:, 0:d["p1"] - d["p0"]],
          start=(d["ci"] == 0), stop=(d["ci"] == 3),
          tile_position=(0, row), inc=self.tpl)
        tpl_hist.append(self.tpl.v)


    def _flush_evac(self, evac_pending):
        E = self.E
        d = evac_pending.pop(0)
        h, b, vs0 = d["h"], d["b"], d["vs0"]
        w = [(self.tpl, d["tplv"])]
        if b == 0:
            E("vector", "tensor_copy", self.O_sb[:, h * 1024:(h + 1) * 1024],
              self.ps[2][:, 0:1024], wait=w, inc=self.vev)
            E("vector", "tensor_copy", self.lrow(h), self.ps[3][0:1, 0:1024], inc=self.vev)
        else:
            E("vector", "tensor_add",
              self.O_sb[:, h * 1024 + vs0:(h + 1) * 1024],
              self.O_sb[:, h * 1024 + vs0:(h + 1) * 1024],
              self.ps[2][:, vs0:1024], wait=w, inc=self.vev)
            E("vector", "tensor_add", self.lrow(h)[:, vs0:],
              self.lrow(h)[:, vs0:], self.ps[3][0:1, vs0:1024], inc=self.vev)

    def lrow(self, h):
        return self.lcarv[h // 3][32 * (h % 3):32 * (h % 3) + 1, :]


    def _ph4(self):
        E = self.E
        dS, dG, cS, cV, cT = self.dS, self.dG, self.cS, self.cV, self.cT
        self.W("vector", [(dS, self.g_q_written_dS)])
        self._prep_finalize()
        tS, vsd, spp, tpl, vev = self.tS, self.vsd, self.spp, self.tpl, self.vev
        ps = self.ps
        smult_hist = []
        sfill_hist = []
        svsd_full = []
        spp_hist = []
        tpl_hist = []
        kk = 0
        blk_tpl = []
        kv_dma_hist = []
        kd_hist = []
        grp_tpl = []

        last_gT_cs = 0

        pending = []
        evac_pending = []
        for g in range(NG):
            nch = 16 * g + 16
            nblk = nch // 4
            w = [(dS, self.g_q_written_dS)]
            if g > 0:
                w.append((tpl, grp_tpl[g - 1]))
            E("scalar", "dma_start", out=self.Qg_v[:],
              in_=self.qS.ap()[g * 1048576:(g + 1) * 1048576],
              wait=w, inc=dS, incv=16)
            qg_dma = dS.v

            for b in range(nblk):
                c0 = b * 4
                gb = len(blk_tpl)
                w = [(dS, self.g_kv_written_dS)]
                if gb >= 2:
                    w.append((tpl, blk_tpl[gb - 2]))
                E("scalar", "dma_start", out=self.Kbuf[gb % 2][:],
                  in_=AP(self.ktS.ap().tensor, c0 * 131072,
                         [[1024, 128], [131072, 4], [1, 1024]]),
                  wait=w, inc=dS, incv=16)
                kd_hist.append(dS.v)
                wv_ = None
                E("scalar", "dma_start", out=self.Vbuf[gb % 2][:],
                  in_=AP(self.vS.ap().tensor, c0 * 131072,
                         [[1024, 128], [131072, 4], [1, 1024]]),
                  wait=wv_, inc=dS, incv=16)
                kv_dma_hist.append(dS.v)

                for h in range(8):
                    npieces_head = 0
                    first_cols = None
                    for ci in range(4):
                        c = c0 + ci
                        jmin = max(0, (c - 16 * g) // 2)
                        vs = jmin * 128
                        jm = (c - 16 * g) // 2
                        q0m = 384 if (c - 16 * g) % 2 == 0 else 256
                        pieces = []
                        if vs < 512:
                            pieces.append((vs, 512))
                        pieces.append((max(512, vs), 1024))
                        if first_cols is None:
                            first_cols = pieces[0][0]
                        for (p0, p1) in pieces:
                            kslot = kk % 4
                            sp = self.ps[kslot // 2][:, (kslot % 2) * 512:(kslot % 2) * 512 + 512]
                            sdb = self.SD[kslot // 2][:, (kslot % 2) * 512:(kslot % 2) * 512 + 512]
                            pbb = self.Pb[kslot // 2][:, (kslot % 2) * 512:(kslot % 2) * 512 + 512]
                            w = [(dS, kd_hist[gb]), (dS, qg_dma)]
                            if kk >= 4:
                                w.append((vsd, smult_hist[kk - 4]))
                            if b == 0 and h == 0 and ci == 0 and g > 0 and p0 == pieces[0][0]:
                                w.append((cV, grp_vec_done))
                            E("tensor", "matmul", sp[:, 0:p1 - p0],
                              self.Kbuf[gb % 2][:, ci * 1024 + h * 128:ci * 1024 + h * 128 + 128],
                              self.Qg_v[:, h * 1024 + p0:h * 1024 + p1],
                              start=True, stop=True, wait=w, inc=tS)
                            sfill_hist.append(tS.v)
                            q0 = (16 * g + 1 - c) * 128 + NEG
                            j0 = p0 // 128
                            nj = (p1 - p0) // 128
                            t2v = self.T2f[:, q0 + j0 * 256:q0 + j0 * 256 + nj * 256]
                            t2v = t2v.rearrange("p (j f) -> p j f", f=256)[:, :, 0:128]
                            sview = sp[:, 0:p1 - p0].rearrange("p (j f) -> p j f", f=128)
                            w = [(tS, sfill_hist[kk])]
                            if kk >= 4:
                                w.append((spp, spp_hist[kk - 4]))
                            E("vector", "tensor_mul",
                              sdb[:, 0:p1 - p0].rearrange("p (j f) -> p j f", f=128),
                              sview, t2v, wait=w, inc=vsd)
                            if p0 <= jm * 128 < p1:
                                E("vector", "tensor_add",
                                  sdb[:, jm * 128 - p0:jm * 128 - p0 + 128],
                                  sdb[:, jm * 128 - p0:jm * 128 - p0 + 128],
                                  self.M2sb[:, q0m:q0m + 128], inc=vsd)
                            smult_hist.append(vsd.v)
                            w = [(vsd, smult_hist[kk])]
                            if kk >= 4:
                                w.append((tpl, tpl_hist[kk - 4]))
                            E("scalar", "activation", pbb[:, 0:p1 - p0], sdb[:, 0:p1 - p0],
                              AF.Exp, wait=w, inc=spp)
                            spp_hist.append(spp.v)
                            pending.append(dict(gb=gb, ci=ci, p0=p0, p1=p1, h=h,
                                                kk=kk, vdma=kv_dma_hist[gb],
                                                first=(ci == 0 and p0 == first_cols)))
                            while len(pending) > 3:
                                self._flush_pv(pending, tpl_hist, spp_hist)
                            kk += 1
                            npieces_head += 1
                            if npieces_head == 2 and evac_pending:
                                self._flush_evac(evac_pending)
                    vs0 = max(0, (c0 - 16 * g) // 2) * 128
                    evac_pending.append(dict(h=h, b=b, vs0=vs0,
                                             tplv=tpl.v + len(pending)))
                blk_tpl.append(tpl.v + len(pending))
            while pending:
                self._flush_pv(pending, tpl_hist, spp_hist)
            while evac_pending:
                self._flush_evac(evac_pending)
            grp_tpl.append(tpl.v)
            grp_vev = vev.v

            if "dbg_o" in self.dbg_outs and g == 0:
                E("gpsimd", "dma_start", out=self.dbg_outs["dbg_o"],
                  in_=self.O_sb, wait=[(vev, grp_vev)], inc=dG, incv=16)
                for h in range(8):
                    E("gpsimd", "dma_start",
                      out=self.dbg_outs["dbg_l"][:, h * 1024:(h + 1) * 1024],
                      in_=self.lrow(h), wait=[(vev, grp_vev)], inc=dG, incv=16)

            stats_free = [None, None]
            bc_free = [None, None]
            gt_hist = []
            sq_big_cs = {}
            for h in range(8):
                pr = h % 2
                rA = self.SD[pr][0:1, :]
                rB = self.tiles[pr][0:1, :]
                rC = self.tiles[2 + pr][0:1, :]
                sqt = self.tiles[2 + pr]
                hc = slice(h * 1024, (h + 1) * 1024)
                w = [(vev, grp_vev)]
                if h >= 2:
                    w.append((cT, sq_free[pr]))
                if h >= 2:
                    w.append((cV, gt_hist[h - 2]))
                E("scalar", "activation", sqt[:], self.O_sb[:, hc], AF.Square,
                  wait=w, inc=cS)
                sq_big_cs[h] = cS.v
                w = [(vev, grp_vev), (cS, sq_big_cs[h])]
                if h >= 1 and stats_free[0] is not None:
                    w.append((cS, stats_free[0]))
                    w.append((cV, stats_free[1]))
                if h == 0:
                    w.append((vsd, smult_hist[kk - 1]))
                for pi, (p0, p1) in enumerate([(0, 512), (512, 1024)]):
                    E("tensor", "matmul", ps[0][0:1, p0:p1], self.o128f_sb[:],
                      self.O_sb[:, h * 1024 + p0:h * 1024 + p1],
                      start=True, stop=True, wait=w if pi == 0 else None)
                for pi, (p0, p1) in enumerate([(0, 512), (512, 1024)]):
                    E("tensor", "matmul", ps[1][0:1, p0:p1], self.o128f_sb[:],
                      sqt[:, p0:p1], start=True, stop=True,
                      inc=cT if pi == 1 else None)
                stats_ct = cT.v
                if h >= 2:
                    pass
                sq_free_prev = sq_free[pr] if h >= 2 else None
                E("scalar", "copy", rC, ps[0][0:1, :], wait=[(cT, stats_ct)], inc=cS)
                E("scalar", "activation", rA, ps[0][0:1, :], AF.Square, inc=cS)
                sq_cs = cS.v
                E("vector", "tensor_sub", rA, ps[1][0:1, :], rA,
                  wait=[(cS, sq_cs)], inc=cV)
                var_cv = cV.v
                stats_free = [sq_cs, var_cv]
                E("scalar", "activation", rB, self.lrow(h), AF.Square, inc=cS)
                lsq_cs = cS.v
                E("vector", "scalar_tensor_tensor", rB, rB, EPS, rA,
                  OP.mult, OP.add, wait=[(cS, lsq_cs), (cV, var_cv)], inc=cV)
                den_cv = cV.v
                E("scalar", "activation", rA, rB, AF.Ln, wait=[(cV, den_cv)], inc=cS)
                E("scalar", "activation", rB, rA, AF.Exp, scale=-0.5, inc=cS)
                rstd_cs = cS.v
                w = [(cS, sq_cs)]
                if bc_free[0] is not None:
                    w.append((cV, bc_free[0]))
                for pi, (p0, p1) in enumerate([(0, 512), (512, 1024)]):
                    E("tensor", "matmul", ps[2][:, p0:p1], self.oc_sb[:],
                      rC[:, p0:p1], start=True, stop=True,
                      wait=w if pi == 0 else None)
                w2 = [(cS, rstd_cs)]
                if bc_free[1] is not None:
                    w2.append((cV, bc_free[1]))
                for pi, (p0, p1) in enumerate([(0, 512), (512, 1024)]):
                    E("tensor", "matmul", ps[3][:, p0:p1], self.oc_sb[:],
                      rB[:, p0:p1], start=True, stop=True,
                      wait=w2 if pi == 0 else None, inc=cT if pi == 1 else None)
                bc_ct = cT.v
                E("vector", "tensor_sub", sqt[:], self.O_sb[:, hc],
                  ps[2][:, 0:1024], wait=[(cT, bc_ct)], inc=cV)
                sub_cv = cV.v
                E("vector", "tensor_mul", sqt[:], sqt[:], ps[3][:, 0:1024], inc=cV)
                mult_cv = cV.v
                bc_free = [sub_cv, mult_cv]
                E("vector", "tensor_scalar", self.GT_v[:, hc], sqt[:],
                  self.gm_sb[:, h:h + 1], self.bt_sb[:, h:h + 1], OP.mult, OP.add,
                  wait=[(self.dT, self.prep_dT), (dG, self.prep_dG)]
                  if (g == 0 and h == 0) else None,
                  inc=cV)
                gt_hist.append(cV.v)
                sq_free = getattr(self, "_gn_sqfree", [None, None])
                sq_free[pr] = stats_ct
                self._gn_sqfree = sq_free
            last_gT_cs = cV.v
            self._gn_sqfree = [None, None]
            woev_hist = []
            prev_x2_cs = None
            prev_dg = 0
            for tt in range(8):
                pk = ps[tt % 2]
                xqt = self.tiles[2 + tt % 2]
                w = []
                if tt >= 2:
                    w.append((cV, woev_hist[tt - 2][0]))
                E("scalar", "dma_start", out=xqt[:],
                  in_=self.xq[(g * 1024 + tt * 128):(g * 1024 + (tt + 1) * 128), :],
                  wait=w if w else None, inc=dS, incv=16)
                xq_dma = dS.v
                w = [(cV, last_gT_cs)]
                if self.lite & 128:
                    w += [(vev, vev.v), (spp, spp.v), (vsd, vsd.v), (tpl, tpl.v),
                          (tS, tS.v), (cS, cS.v), (dS, dS.v), (dG, self.dG.v)]
                if tt >= 2:
                    w.append((cV, woev_hist[tt - 2][1]))
                first = True
                for hh in range(8):
                    for oc_i in range(2):
                        E("tensor", "matmul", pk[:, oc_i * 512:(oc_i + 1) * 512],
                          self.GT_v[:, hh * 1024 + tt * 128:hh * 1024 + tt * 128 + 128],
                          self.Wo_sb[:, hh * 1024 + oc_i * 512:hh * 1024 + oc_i * 512 + 512],
                          start=(hh == 0), stop=(hh == 7),
                          wait=w if first else None,
                          inc=cT if (hh == 7 and oc_i == 1) else None)
                        first = False
                wo_ct = cT.v
                wv = [(cT, wo_ct), (dS, xq_dma)]
                if prev_x2_cs is not None:
                    wv.append((cS, prev_x2_cs))
                    wv.append((dG, prev_dg))
                E("vector", "tensor_add", self.tiles[1][:], pk[:, 0:1024],
                  self.bo_rep[:], wait=wv, inc=cV)
                E("vector", "tensor_add", self.tiles[0][:], self.tiles[1][:],
                  xqt[:], inc=cV)
                ev0 = cV.v
                dbg_wait = []
                if "dbg_a" in self.dbg_outs and g == 0:
                    E("gpsimd", "dma_start",
                      out=self.dbg_outs["dbg_a"][:, tt * 1024:(tt + 1) * 1024],
                      in_=self.tiles[0][:], wait=[(cV, ev0)], inc=dG, incv=16)
                    dbg_wait = [(dG, dG.v)]
                E("vector", "reduce_sum", self.colA[:], self.tiles[0][:], axis=AX.X, inc=cV)
                E("vector", "tensor_scalar_mul", self.colA[:], self.colA[:],
                  -1.0 / 1024, wait=[(cV, cV.v)], inc=cV)
                neg_cv = cV.v
                E("scalar", "add", self.tiles[1][:], self.tiles[0][:],
                  self.colA[:, 0:1], wait=[(cV, neg_cv)] + dbg_wait, inc=cS)
                c_cs = cS.v
                E("vector", "tensor_mul", self.tiles[0][:], self.tiles[1][:],
                  self.tiles[1][:], wait=[(cS, c_cs)], inc=cV)
                E("vector", "reduce_sum", self.colB[:], self.tiles[0][:], axis=AX.X, inc=cV)
                E("vector", "tensor_scalar", self.colB[:], self.colB[:],
                  1.0 / 1024, EPS, OP.mult, OP.add, wait=[(cV, cV.v)], inc=cV)
                var_cv = cV.v
                dwait2 = []
                if "dbg_l" in self.dbg_outs and g == 0 and tt == 0:
                    E("gpsimd", "dma_start", out=AP(self.dbg_outs["dbg_l"].tensor, 0, [[1, 128], [1, 1]]),
                      in_=self.colA[:], wait=[(cV, neg_cv), (cS, cS.v)], inc=dG, incv=16)
                    E("gpsimd", "dma_start", out=AP(self.dbg_outs["dbg_l"].tensor, 128, [[1, 128], [1, 1]]),
                      in_=self.colB[:], wait=[(cV, var_cv)], inc=dG, incv=16)
                    dwait2 = [(dG, dG.v)]
                E("scalar", "activation", self.colA[:], self.colB[:], AF.Sqrt,
                  wait=[(cV, var_cv)] + dwait2, inc=cS)
                E("vector", "reciprocal", self.colB[:], self.colA[:],
                  wait=[(cS, cS.v)], inc=cV)
                rstd_cv = cV.v
                if "dbg_l" in self.dbg_outs and g == 0 and tt == 0:
                    E("gpsimd", "dma_start", out=AP(self.dbg_outs["dbg_l"].tensor, 256, [[1, 128], [1, 1]]),
                      in_=self.colB[:], wait=[(cV, rstd_cv)], inc=dG, incv=16)
                    dwait3 = [(dG, dG.v)]
                else:
                    dwait3 = []
                E("scalar", "mul", self.tiles[0][:], self.tiles[1][:],
                  self.colB[:, 0:1], wait=[(cV, rstd_cv)] + dwait3, inc=cS)
                x2_cs = cS.v
                E("vector", "tensor_copy", self.x2bt[:], self.tiles[0][:],
                  wait=[(cS, x2_cs), (dG, dG.v)], inc=cV)
                ev1 = cV.v
                row0 = g * 1024 + tt * 128
                E("gpsimd", "dma_start", out=self.x2f.ap()[row0:row0 + 128, :],
                  in_=self.tiles[0][:], wait=[(cV, ev1)], inc=dG, incv=16)
                E("gpsimd", "dma_start", out=self.x2b.ap()[row0:row0 + 128, :],
                  in_=self.x2bt[:], wait=[(cV, ev1)], inc=dG, incv=16)
                woev_hist.append((ev0, ev1))
                prev_x2_cs = x2_cs
                prev_dg = dG.v
            grp_vec_done = cV.v
        self.g_x2_written = dG.v
        if "dbg_x2" in self.dbg_outs:
            E("gpsimd", "dma_start", out=self.dbg_outs["dbg_x2"], in_=self.x2f.ap(),
              wait=[(dG, self.g_x2_written)], inc=dG, incv=16)


    def _ph5(self):
        E = self.E
        dS, dG, dT, dO = self.dS, self.dG, self.dT, self.dO
        cS, cV, cT = self.cS, self.cV, self.cT
        ps = self.ps
        wb = lambda s: self.Wo_sb[:, s * 4096:(s + 1) * 4096]

        ffnA = self.tiles[2]
        ffnB = self.tiles[3]
        ph4_ct = cT.v
        wdma_hist = [ph4_ct, ph4_ct]
        ab_mult_hist = cV.v
        ln2_hist = [(cS.v, 0), (cS.v, 0)]
        last_w3_cv = cV.v

        prev_w3_region = [cV.v] * 4
        for ch in range(4):
            w = [(dG, self.g_x2_written)]
            for ib in range(8):
                E("sync", "dma_start", out=self.X2T_v[:, ib * 1024:(ib + 1) * 1024],
                  in_=self.x2b.ap()[ch * 1024:(ch + 1) * 1024, ib * 128:(ib + 1) * 128],
                  transpose=True, wait=w if ib == 0 else None, inc=dT, incv=16)
            x2t_dT = dT.v

            for ob in range(32):
                slot = ob % 2
                E("sync", "dma_start", out=wb(slot)[:, 0:1024],
                  in_=self.wts["w1"].ap()[ob * 131072:(ob + 1) * 131072],
                  wait=[(cT, wdma_hist[slot])], inc=dT, incv=16)
                E("sync", "dma_start", out=wb(slot)[:, 1024:2048],
                  in_=self.wts["w2"].ap()[ob * 131072:(ob + 1) * 131072],
                  inc=dT, incv=16)
                wdT = dT.v
                fills = []
                for wi in range(2):
                    pk, pw = self._psum_fill_wait()
                    pw = pw + [(dT, wdT), (dT, x2t_dT)]
                    if ch == 0 and ob == 0 and wi == 0:
                        pw.append((cV, ab_mult_hist))
                    if ch > 0 and ob < 2:
                        pw.append((cV, prev_w3_region[self.ps.index(pk)]))
                    for tp in range(2):
                        for ib in range(8):
                            E("tensor", "matmul", pk[:, tp * 512:(tp + 1) * 512],
                              wb(slot)[:, wi * 1024 + ib * 128:wi * 1024 + ib * 128 + 128],
                              self.X2T_v[:, ib * 1024 + tp * 512:ib * 1024 + tp * 512 + 512],
                              start=(ib == 0), stop=(ib == 7),
                              wait=pw if (tp == 0 and ib == 0) else None,
                              inc=cT if (tp == 1 and ib == 7) else None)
                    fills.append((pk, cT.v))
                wdma_hist[slot] = cT.v
                w = [(cT, fills[0][1]), (cV, ab_mult_hist)]
                E("scalar", "activation", ffnA[:], fills[0][0][:, 0:1024], AF.Silu,
                  scale=self.s_sb[:, ob:ob + 1], wait=w, inc=cS)
                self.evac_hist.append(cS.v)
                E("scalar", "mul", ffnB[:], fills[1][0][:, 0:1024],
                  self.s_sb[:, 32 + ob:32 + ob + 1], wait=[(cT, fills[1][1])], inc=cS)
                self.evac_hist.append(cS.v)
                E("vector", "tensor_mul", self.A_v[:, ob * 1024:(ob + 1) * 1024],
                  ffnA[:], ffnB[:], wait=[(cS, cS.v)], inc=cV)
                ab_mult_hist = cV.v
            a_done_cv = cV.v

            h_done_cs = 0
            for hname, src_v, dst_v, scol in [("h1", self.A_v, self.B_v, 64),
                                              ("h2", self.B_v, self.A_v, 96)]:
                for ob in range(32):
                    slot = ob % 2
                    E("sync", "dma_start", out=wb(slot)[:],
                      in_=self.wts[hname].ap()[ob * 524288:(ob + 1) * 524288],
                      wait=[(cT, wdma_hist[slot])], inc=dT, incv=16)
                    wdT = dT.v
                    pk, pw = self._psum_fill_wait()
                    pw = pw + [(dT, wdT)]
                    if ob == 0:
                        if hname == "h1":
                            pw.append((cV, a_done_cv))
                        else:
                            pw.append((cS, h_done_cs))
                    for tp in range(2):
                        for ib in range(32):
                            E("tensor", "matmul", pk[:, tp * 512:(tp + 1) * 512],
                              wb(slot)[:, ib * 128:ib * 128 + 128],
                              src_v[:, ib * 1024 + tp * 512:ib * 1024 + tp * 512 + 512],
                              start=(ib == 0), stop=(ib == 31),
                              wait=pw if (tp == 0 and ib == 0) else None,
                              inc=cT if (tp == 1 and ib == 31) else None)
                    wdma_hist[slot] = cT.v
                    E("scalar", "activation", dst_v[:, ob * 1024:(ob + 1) * 1024],
                      pk[:, 0:1024], AF.Gelu, scale=self.s_sb[:, scol + ob:scol + ob + 1],
                      wait=[(cT, cT.v)], inc=cS)
                    self.evac_hist.append(cS.v)
                h_done_cs = cS.v

            h2_mm_ct = cT.v
            for j in range(8):
                loc0 = ch * 1024 + j * 128
                E("scalar", "dma_start", out=self.Bf_v[:, j * 1024:(j + 1) * 1024],
                  in_=self.x2f.ap()[loc0:loc0 + 128, :],
                  wait=[(dG, self.g_x2_written), (cT, h2_mm_ct)] if j == 0 else None,
                  inc=dS, incv=16)
            x2r_dS = dS.v
            w3_region_cv = [cV.v] * 4
            for half in range(2):
                for ib in range(32):
                    slot = ib % 2
                    E("sync", "dma_start", out=wb(slot)[:, 0:1024],
                      in_=self.wts["w3"].ap()[ib * 131072:(ib + 1) * 131072],
                      wait=[(cS, h_done_cs), (cT, wdma_hist[slot])], inc=dT, incv=16)
                    wdT = dT.v
                    for tt in range(4):
                        pk = ps[tt]
                        pw = None
                        if ib == 0:
                            pw = [(dT, wdT), (cS, self.evac_hist[-1])]
                            if half == 1:
                                pw.append((cV, w3_region_cv[tt]))
                        elif tt == 0:
                            pw = [(dT, wdT)]
                        for oc_i in range(2):
                            E("tensor", "matmul", pk[:, oc_i * 512:(oc_i + 1) * 512],
                              self.A_v[:, ib * 1024 + (half * 4 + tt) * 128:
                                       ib * 1024 + (half * 4 + tt) * 128 + 128],
                              wb(slot)[:, oc_i * 512:(oc_i + 1) * 512],
                              start=(ib == 0), stop=(ib == 31),
                              wait=pw if oc_i == 0 else None,
                              inc=cT if (ib == 31 and tt == 3 and oc_i == 1) else None)
                w3_ct = cT.v
                wdma_hist[0] = w3_ct
                wdma_hist[1] = w3_ct
                for tt in range(4):
                    pk = ps[tt]
                    pr = tt % 2
                    R2 = self.tiles[0 + pr]
                    C2 = self.tiles[2 + pr]
                    cX = self.colA if pr == 0 else self.colC
                    cY = self.colB if pr == 0 else self.colD
                    j = half * 4 + tt
                    loc0 = ch * 1024 + j * 128
                    wv = [(cT, w3_ct), (dS, x2r_dS)]
                    if ln2_hist[pr][0] is not None:
                        wv.append((cS, ln2_hist[pr][0]))
                        wv.append((dO, ln2_hist[pr][1]))
                    E("vector", "tensor_mul", R2[:], pk[:, 0:1024],
                      self.w3s_rep[:], wait=wv, inc=cV)
                    E("vector", "tensor_add", R2[:], R2[:],
                      self.Bf_v[:, j * 1024:(j + 1) * 1024], inc=cV)
                    w3_region_cv[tt] = cV.v
                    E("vector", "reduce_sum", cX[:], R2[:], axis=AX.X, inc=cV)
                    E("vector", "tensor_scalar_mul", cX[:], cX[:],
                      -1.0 / 1024, wait=[(cV, cV.v)], inc=cV)
                    neg_cv = cV.v
                    E("scalar", "add", C2[:], R2[:], cX[:, 0:1],
                      wait=[(cV, neg_cv)], inc=cS)
                    c_cs = cS.v
                    E("vector", "tensor_mul", R2[:], C2[:], C2[:],
                      wait=[(cS, c_cs)], inc=cV)
                    E("vector", "reduce_sum", cY[:], R2[:], axis=AX.X, inc=cV)
                    E("vector", "tensor_scalar", cY[:], cY[:],
                      1.0 / 1024, EPS, OP.mult, OP.add, wait=[(cV, cV.v)], inc=cV)
                    var_cv = cV.v
                    E("scalar", "activation", cX[:], cY[:], AF.Sqrt,
                      wait=[(cV, var_cv)], inc=cS)
                    E("vector", "reciprocal", cY[:], cX[:],
                      wait=[(cS, cS.v)], inc=cV)
                    rstd_cv = cV.v
                    E("scalar", "mul", R2[:], C2[:], cY[:, 0:1],
                      wait=[(cV, rstd_cv)], inc=cS)
                    x2_cs = cS.v
                    E("gpsimd", "dma_start", out=self.y[loc0:loc0 + 128, :],
                      in_=R2[:], wait=[(cS, x2_cs)], inc=dO, incv=16)
                    ln2_hist[pr] = (x2_cs, dO.v)
            prev_w3_region = list(w3_region_cv)

    def _finish(self):
        if self.lite & 64:
            return
        self.W("gpsimd", [(self.dG, self.dG.v)])
        self.W("sync", [(self.dT, self.dT.v)])
        self.W("scalar", [(self.dS, self.dS.v), (self.dC, self.c_consts), (self.dO, self.dO.v)])

    def run(self, in_maps, core_ids, **kw):
        with self.st:
            with self.nc.Block() as block:
                @block.tensor
                def _(e):
                    for fn in self.plan.q["tensor"]:
                        fn(e)

                @block.vector
                def _(e):
                    for fn in self.plan.q["vector"]:
                        fn(e)

                @block.scalar
                def _(e):
                    for fn in self.plan.q["scalar"]:
                        fn(e)

                @block.sync
                def _(e):
                    for fn in self.plan.q["sync"]:
                        fn(e)

                @block.gpsimd
                def _(e):
                    for fn in self.plan.q["gpsimd"]:
                        fn(e)
            return run_bass_kernel_spmd(self.nc, in_maps, core_ids=list(core_ids), **kw)



def make_core_inputs(inputs, b, p):
    X = np.asarray(inputs["L0_attn_x"])
    shift = -128 * (1 - p)
    n = np.arange(NQ, dtype=np.float64) - NEG + shift
    nramp = np.tile(np.maximum(n, 0.0).astype(np.float32), (8, 1))
    ind = (n >= 0).astype(np.float32)[None, :]
    pp, qq = np.mgrid[0:128, 0:640]
    m2 = np.where(qq - pp - 256 + shift < 0, np.float32(NEGINF), np.float32(0.0))
    xfull = np.ascontiguousarray(X[b])
    xq = np.ascontiguousarray(X[b].reshape(64, 128, D)[p::2].reshape(TOK, D))
    return {
        "x": xfull, "xq": xq,
        "wqm": np.ascontiguousarray(inputs["L0_attn_Wq"]),
        "wkm": np.ascontiguousarray(inputs["L0_attn_Wk"]),
        "wvm": np.ascontiguousarray(inputs["L0_attn_Wv"]),
        "wom": np.ascontiguousarray(inputs["L0_attn_Wo"]),
        "bo": np.asarray(inputs["L0_attn_bo"], np.float32).reshape(1, D),
        "theta": np.asarray(inputs["L0_attn_theta"], np.float32).reshape(8, 1),
        "gamma": np.ascontiguousarray(np.asarray(inputs["L0_attn_gamma"], np.float32).reshape(8, 128).T),
        "beta": np.ascontiguousarray(np.asarray(inputs["L0_attn_beta"], np.float32).reshape(8, 128).T),
        "w1m": np.ascontiguousarray(inputs["L0_ffn_w1_w"]),
        "s1": np.ascontiguousarray(np.asarray(inputs["L0_ffn_w1_s"], np.float32).reshape(32, 128).T),
        "w2m": np.ascontiguousarray(inputs["L0_ffn_w2_w"]),
        "s2": np.ascontiguousarray(np.asarray(inputs["L0_ffn_w2_s"], np.float32).reshape(32, 128).T),
        "h1m": np.ascontiguousarray(inputs["L0_ffn_h1_w"]),
        "sh1": np.ascontiguousarray(np.asarray(inputs["L0_ffn_h1_s"], np.float32).reshape(32, 128).T),
        "h2m": np.ascontiguousarray(inputs["L0_ffn_h2_w"]),
        "sh2": np.ascontiguousarray(np.asarray(inputs["L0_ffn_h2_s"], np.float32).reshape(32, 128).T),
        "w3m": np.ascontiguousarray(inputs["L0_ffn_w3_w"]),
        "w3s": np.asarray(inputs["L0_ffn_w3_s"], np.float32).reshape(1, D),
        "nramp": nramp, "ind": ind, "m2": m2,
        "o8": np.full((8, 1), 0.125, np.float32),
        "oc": np.ones((1, 128), np.float32),
        "o128f": np.full((128, 1), 1.0 / 128, np.float32),
        "o128b": np.ones((128, 1), ml_dtypes.bfloat16),
    }


def kernel(**inputs):
    b = Builder()
    in_maps = [make_core_inputs(inputs, c // 2, c % 2) for c in range(8)]
    res = b.run(in_maps, core_ids=list(range(8)))
    X = np.asarray(inputs["L0_attn_x"])
    out = np.empty((4, T, D), np.float32)
    for c in range(8):
        bb, p = c // 2, c % 2
        out[bb].reshape(64, 128, D)[p::2] = res.results[c]["y"].reshape(32, 128, D)
    return out
